# version 11
# baseline (speedup 1.0000x reference)
import contextlib
import numpy as np
import ml_dtypes
import concourse.bass as bass
import concourse.mybir as mybir
from concourse.bass_core_rust import IndirectOffsetOnAxis
from concourse.bass_utils import run_bass_kernel_spmd

F32 = mybir.dt.float32
BF16 = mybir.dt.bfloat16
I32 = mybir.dt.int32
AF = mybir.ActivationFunctionType
OP = mybir.AluOpType
AX = mybir.AxisListType

B, L, D = 32, 2048, 1024
NCORES = 8
NSEQ = B // NCORES
W = 512
NCH = L // W
NH, DH = 16, 64
DFF, NE = 4096, 8
QSCALE = 0.125
EPS = 1e-5
KC = D // 128
TOK = NSEQ * L
NG = TOK // 128
GSEQ = L // 128
VROW = NH * 65
CAPS = 1152
CAPE = 4 * CAPS
NBLK = CAPE // 128
GMAGIC = float(1 << 20)
FFN_MODE = "sparse"

DEV_TAPS = False


class Prog:

    ENGINES = ("sync", "scalar", "vector", "tensor", "gpsimd")

    def __init__(self, nc):
        self.nc = nc
        self.ops = {e: [] for e in self.ENGINES}
        self.cnt = {}
        self.sems = {}
        self._ctx = []

    def sem(self, name):
        ctx = self.nc.semaphore(name)
        self.sems[name] = ctx.__enter__()
        self._ctx.append(ctx)
        self.cnt[name] = 0
        return name

    def now(self, *names):
        return [(n, self.cnt[n]) for n in names]

    def op(self, eng, build, waits=(), inc=None, inc_n=1):
        waits = tuple((s, v) for (s, v) in waits if v > 0)
        sem = self.sems[inc] if inc is not None else None

        def fn(e, build=build, waits=waits, sem=sem, inc_n=inc_n):
            for (s, v) in waits:
                e.wait_ge(self.sems[s], v)
            ins = build(e)
            if sem is not None:
                ins.then_inc(sem, inc_n)

        if inc is not None:
            self.cnt[inc] += inc_n
        self.ops[eng].append(fn)

    def dma(self, eng, out, in_, sem, waits=(), **kw):
        self.op(eng, lambda e: e.dma_start(out, in_, **kw), waits=waits,
                inc=sem, inc_n=16)

    def run(self, block):
        for name in self.ENGINES:
            if self.ops[name]:
                body = self.ops[name]

                def make(body=body):
                    def _(eng):
                        for fn in body:
                            fn(eng)
                    return _

                getattr(block, name)(make())

    def finish(self):
        for ctx in reversed(self._ctx):
            ctx.__exit__(None, None, None)


def build_core_program(n_seq=NSEQ, dev_taps=DEV_TAPS):
    nc = bass.Bass("TRN2", target_bir_lowering=False)
    tok = n_seq * L
    ng = tok // 128

    xT = nc.dram_tensor("xT", [n_seq, D, L], F32, kind="ExternalInput").ap()
    wqT = nc.dram_tensor("wqT", [D, D], F32, kind="ExternalInput").ap()
    wkT = nc.dram_tensor("wkT", [D, D], F32, kind="ExternalInput").ap()
    wvT = nc.dram_tensor("wvT", [D, D], F32, kind="ExternalInput").ap()
    woMC = nc.dram_tensor("woMC", [KC, DH, NH, 128], F32, kind="ExternalInput").ap()
    bq = nc.dram_tensor("bq", [128, KC], F32, kind="ExternalInput").ap()
    bk = nc.dram_tensor("bk", [128, KC], F32, kind="ExternalInput").ap()
    bvC = nc.dram_tensor("bvC", [64, NH], F32, kind="ExternalInput").ap()
    bo = nc.dram_tensor("bo", [128, KC], F32, kind="ExternalInput").ap()
    ltriBD = nc.dram_tensor("ltriBD", [128, 128], F32, kind="ExternalInput").ap()
    wrT = nc.dram_tensor("wrT", [D, NE], F32, kind="ExternalInput").ap()
    w1T = nc.dram_tensor("w1T", [NE, D, DFF], F32, kind="ExternalInput").ap()
    b1 = nc.dram_tensor("b1", [128, NE, 32], F32, kind="ExternalInput").ap()
    w2T = nc.dram_tensor("w2T", [NE, DFF, D], F32, kind="ExternalInput").ap()
    b2 = nc.dram_tensor("b2", [128, NE, KC], F32, kind="ExternalInput").ap()
    b2r = nc.dram_tensor("b2r", [NE, D], F32, kind="ExternalInput").ap()
    outT = nc.dram_tensor("outT", [n_seq, D, L], F32, kind="ExternalOutput").ap()

    xmid_d = nc.dram_tensor("xmid_d", [n_seq, D, L], F32).ap()
    ln2tok_d = nc.dram_tensor("ln2tok_d", [tok, D], BF16).ap()
    wobf = nc.dram_tensor("wobf", [KC, DH, NH, 128], BF16).ap()
    w1bf = nc.dram_tensor("w1bf", [NE, D, DFF], BF16).ap()
    w2bf = nc.dram_tensor("w2bf", [NE, DFF, D], BF16).ap()
    g_d = nc.dram_tensor("g_d", [NE * n_seq * CAPS, 1], F32).ap()
    slot_d = nc.dram_tensor("slot_d", [2 * tok + 128, D], BF16).ap()

    taps = {}
    if dev_taps:
        taps["t_xmid"] = nc.dram_tensor("t_xmid", [n_seq, D, L], F32,
                                        kind="ExternalOutput").ap()
        taps["t_logits"] = nc.dram_tensor("t_logits", [128, ng * NE], F32,
                                          kind="ExternalOutput").ap()
        taps["t_wt"] = nc.dram_tensor("t_wt", [128, NE * NG], F32,
                                      kind="ExternalOutput").ap()
        taps["t_g"] = nc.dram_tensor("t_g", [128, NE * n_seq * CAPS // 128], F32,
                                     kind="ExternalOutput").ap()

    P = Prog(nc)
    s_x = P.sem("s_x")
    s_w = P.sem("s_w")
    s_wc = P.sem("s_wc")
    s_wf = P.sem("s_wf")
    s_g = P.sem("s_g")
    s_pe = P.sem("s_pe")
    s_act = P.sem("s_act")
    s_dve = P.sem("s_dve")
    s_st = P.sem("s_st")
    s_out = P.sem("s_out")

    bank_marks = [()] * 8

    with contextlib.ExitStack() as _es:
        ent = _es.enter_context
        ident_f = ent(nc.sbuf_tensor("ident_f", [128, 128], F32))
        ident_bf = ent(nc.sbuf_tensor("ident_bf", [128, 128], BF16))
        ones_col_bf = ent(nc.sbuf_tensor("ones_col_bf", [128, 1], BF16))
        ones_row_bf = ent(nc.sbuf_tensor("ones_row_bf", [1, 128], BF16))
        b1_sb = ent(nc.sbuf_tensor("b1_sb", [128, NE * 32], F32))
        b2_sb = ent(nc.sbuf_tensor("b2_sb", [128, NE * KC], F32))
        st_sb = ent(nc.sbuf_tensor("st_sb", [1, 3 * W], F32))
        stbf_sb = ent(nc.sbuf_tensor("stbf_sb", [1, 2 * W + 4], BF16))
        sm_sb = ent(nc.sbuf_tensor("sm_sb", [128, 4], F32))
        eps_sb = ent(nc.sbuf_tensor("eps_sb", [128, 1], F32))
        wv12_sb = ent(nc.sbuf_tensor("wv12_sb", [128, 2 * NG], F32))
        PBh = [ent(nc.psum_tensor(f"p{i}", [128, 512], F32)) for i in range(7)]
        p7 = ent(nc.psum_tensor("p7", [128, 1024], BF16))
        p0, p1, p2, p3, p4, p5, p6 = PBh
        blk = ent(nc.Block())
        (lambda: None)()
        PB = PBh

        def bfree(i):
            return list(bank_marks[i])

        def bmark(i, sem):
            bank_marks[i] = P.now(sem)

        nc.const_aps.register(F32, EPS, eps_sb[:, 0:1])
        P.op("vector", lambda e: e.memset(eps_sb[:, :], EPS), inc=s_dve)
        P.op("vector", lambda e: e.memset(ident_f[:, :], 1.0), inc=s_dve)
        P.op("gpsimd", lambda e: e.affine_select(
            ident_f[:, :], ident_f[:, :], [[-1, 128]], OP.is_equal, 0.0,
            base=0, channel_multiplier=1), waits=P.now(s_dve), inc=s_w, inc_n=1)
        P.op("vector", lambda e: e.memset(ident_bf[:, :], 1.0), inc=s_dve)
        P.op("gpsimd", lambda e: e.affine_select(
            ident_bf[:, :], ident_bf[:, :], [[-1, 128]], OP.is_equal, 0.0,
            base=0, channel_multiplier=1), waits=P.now(s_dve), inc=s_w, inc_n=1)
        P.op("vector", lambda e: e.memset(ones_col_bf[:, :], 1.0), inc=s_dve)
        P.op("vector", lambda e: e.memset(ones_row_bf[:, :], 1.0), inc=s_dve)
        P.dma("gpsimd", wobf, woMC, s_wc)
        wocast_mark = P.now(s_wc)
        cast_marks = {}
        for eidx in range(NE):
            P.dma("gpsimd", w1bf[eidx], w1T[eidx], s_wc)
            P.dma("gpsimd", w2bf[eidx], w2T[eidx], s_wc)
            cast_marks[eidx] = P.now(s_wc)
        P.dma("sync", b1_sb[:, :].rearrange("p (e mc) -> p e mc", e=NE), b1, s_x)
        P.dma("sync", b2_sb[:, :].rearrange("p (e mc) -> p e mc", e=NE), b2, s_x)

        with contextlib.ExitStack() as _as:
            aent = _as.enter_context
            wq_sb = aent(nc.sbuf_tensor("wq_sb", [128, KC * D], BF16))
            wk_sb = aent(nc.sbuf_tensor("wk_sb", [128, KC * D], BF16))
            wv_sb = aent(nc.sbuf_tensor("wv_sb", [128, KC * D], BF16))
            wos_sb = aent(nc.sbuf_tensor("wos_sb", [64, 2 * NH * 128], BF16))
            wr_sb = aent(nc.sbuf_tensor("wr_sb", [128, KC * NE], BF16))
            bq_sb = aent(nc.sbuf_tensor("bq_sb", [128, KC], F32))
            bk_sb = aent(nc.sbuf_tensor("bk_sb", [128, KC], F32))
            bo_sb = aent(nc.sbuf_tensor("bo_sb", [128, KC], F32))
            bvc_sb = aent(nc.sbuf_tensor("bvc_sb", [64, NH], F32))
            x_sb = aent(nc.sbuf_tensor("x_sb", [128, 2 * KC * W], F32))
            xbf_sb = aent(nc.sbuf_tensor("xbf_sb", [128, KC * W], BF16))
            a_sb = aent(nc.sbuf_tensor("a_sb", [128, 2 * KC * W], BF16))
            xln_sb = aent(nc.sbuf_tensor("xln_sb", [128, KC * W], BF16))
            k_sb = aent(nc.sbuf_tensor("k_sb", [128, 2 * KC * W], BF16))
            v_sb = aent(nc.sbuf_tensor("v_sb", [128, 2 * 4 * VROW], BF16))
            o_sb = aent(nc.sbuf_tensor("o_sb", [64, NH * W], BF16))
            rbc_sb = aent(nc.sbuf_tensor("rbc_sb", [64, W], BF16))
            lt2_sb = aent(nc.sbuf_tensor("lt2_sb", [128, 2 * 1024], BF16))
            logits_sb = aent(nc.sbuf_tensor("logits_sb", [128, NG * NE], F32))
            rt_sb = aent(nc.sbuf_tensor("rt_sb", [128, 5 * 128 + 64], F32))
            ltbd_sb = aent(nc.sbuf_tensor("ltbd_sb", [128, 128], BF16))
            ptri_sb = aent(nc.sbuf_tensor("ptri_sb", [128, 128], F32))
            evals_sb = aent(nc.sbuf_tensor("evals_sb", [128, NE], F32))
            ids_sb = aent(nc.sbuf_tensor("ids_sb", [128, NG], F32))
            iti_sb = aent(nc.sbuf_tensor("iti_sb", [128, NG], I32))
            posall_sb = aent(nc.sbuf_tensor("posall_sb", [128, 16 * NE], F32))
            fe_all = aent(nc.sbuf_tensor("fe_all", [128, 128], BF16))
            ft_all = aent(nc.sbuf_tensor("ft_all", [128, 128], BF16))
            ct_all = aent(nc.sbuf_tensor("ct_all", [128, 128], F32))
            pm_all = aent(nc.sbuf_tensor("pm_all", [128, 128], F32))
            tot_sb = aent(nc.sbuf_tensor("tot_sb", [128, NE], F32))
            bcol8_sb = aent(nc.sbuf_tensor("bcol8_sb", [128, NE], F32))
            offi_sb = aent(nc.sbuf_tensor("offi_sb", [128, 32], I32))
            valb_sb = aent(nc.sbuf_tensor("valb_sb", [128, 16], F32))
            q_sb = xbf_sb
            ln2_sb = xln_sb

            for wsb, wdr in ((wq_sb, wqT), (wk_sb, wkT), (wv_sb, wvT)):
                P.dma("gpsimd", wsb[:, :].rearrange("p (kc m) -> p kc m", kc=KC),
                      wdr.rearrange("(kc p) m -> p kc m", p=128), s_w)
            P.dma("gpsimd", wr_sb[:, :].rearrange("p (kc e) -> p kc e", kc=KC),
                  wrT.rearrange("(kc p) e -> p kc e", p=128), s_w)
            for bsb, bdr in ((bq_sb, bq), (bk_sb, bk), (bo_sb, bo), (bvc_sb, bvC)):
                P.dma("sync", bsb[:, :], bdr, s_x)
            P.op("vector", lambda e: e.tensor_scalar_mul(
                bq_sb[:, :], bq_sb[:, :], QSCALE), waits=P.now(s_x), inc=s_dve)
            for r in range(2):
                ap = v_sb[:, r * 4 * VROW:(r + 1) * 4 * VROW].rearrange(
                    "p (x d) -> p x d", d=65)[:, :, 64:65]
                P.op("vector", lambda e, ap=ap: e.memset(ap, 1.0), inc=s_dve)
            P.dma("gpsimd", ltbd_sb[:, :], ltriBD, s_w)
            P.op("vector", lambda e: e.memset(ptri_sb[:, :], 1.0), inc=s_dve)
            P.op("gpsimd", lambda e: e.affine_select(
                ptri_sb[:, :], ptri_sb[:, :], [[1, 128]], OP.is_ge, 0.0,
                base=-1, channel_multiplier=-1), waits=P.now(s_dve), inc=s_w, inc_n=1)
            P.op("gpsimd", lambda e: e.iota(iti_sb[:, 0:NE], [[1, NE]], base=0,
                                            channel_multiplier=0), inc=s_w, inc_n=1)
            P.op("vector", lambda e: e.tensor_copy(evals_sb[:, :], iti_sb[:, 0:NE]),
                 waits=P.now(s_w), inc=s_dve)
            P.op("gpsimd", lambda e: e.iota(iti_sb[:, :], [[128, NG]], base=0,
                                            channel_multiplier=1),
                 waits=P.now(s_dve), inc=s_w, inc_n=1)
            P.op("vector", lambda e: e.tensor_copy(ids_sb[:, :], iti_sb[:, :]),
                 waits=P.now(s_w), inc=s_dve)
            ginit_cols = NE * n_seq * CAPS // 128
            P.op("vector", lambda e: e.memset(logits_sb[:, 0:ginit_cols], GMAGIC),
                 inc=s_dve)
            P.dma("sync", g_d.rearrange("(p n) one -> p (n one)", p=128),
                  logits_sb[:, 0:ginit_cols], s_st, waits=P.now(s_dve))
            wginit = P.now(s_st)
            attw = P.now(s_w, s_x, s_dve, s_pe)

            def xsl(ring, kc, a=0, b=W):
                off = ring * KC * W + kc * W
                return x_sb[:, off + a: off + b]

            xring_free = [[], []]

            def emit_xload(s, c):
                ring = (s * NCH + c) % 2
                w = list(xring_free[ring])
                P.dma("sync", x_sb[:, ring * KC * W:(ring + 1) * KC * W]
                      .rearrange("p (kc n) -> p kc n", kc=KC),
                      xT[s, :, c * W:(c + 1) * W].rearrange("(kc p) n -> p kc n", p=128),
                      s_x, waits=w)
                return P.now(s_x)

            loads = {(0, 0): emit_xload(0, 0)}

            def emit_ln(src_slices, out_sb, wsrc, wrows=None):
                if wrows is None:
                    wrows = P.now(s_act, s_pe)
                for kc in range(KC):
                    P.op("vector", lambda e, kc=kc: e.tensor_copy(
                        xbf_sb[:, kc * W:(kc + 1) * W], src_slices(kc)),
                        waits=wsrc if kc == 0 else (), inc=s_dve)
                wcp = P.now(s_dve)
                for kc in range(KC):
                    P.op("tensor", lambda e, kc=kc: e.matmul(
                        p5[0:1, :], ones_col_bf[:, 0:1],
                        xbf_sb[:, kc * W:(kc + 1) * W],
                        start=(kc == 0), stop=(kc == KC - 1)),
                        waits=(wcp + bfree(5)) if kc == 0 else (), inc=s_pe)
                wsum = P.now(s_pe)
                for kc in range(KC):
                    P.op("vector", lambda e, kc=kc: e.tensor_mul(
                        xbf_sb[:, kc * W:(kc + 1) * W],
                        xbf_sb[:, kc * W:(kc + 1) * W],
                        xbf_sb[:, kc * W:(kc + 1) * W]),
                        waits=wsum if kc == 0 else (), inc=s_dve)
                wsq = P.now(s_dve)
                for kc in range(KC):
                    P.op("tensor", lambda e, kc=kc: e.matmul(
                        p6[0:1, :], ones_col_bf[:, 0:1],
                        xbf_sb[:, kc * W:(kc + 1) * W],
                        start=(kc == 0), stop=(kc == KC - 1)),
                        waits=(wsq + bfree(6)) if kc == 0 else (), inc=s_pe)
                wst = P.now(s_pe)
                mrow = st_sb[0:1, 0:W]
                vrow = st_sb[0:1, W:2 * W]
                rrow = st_sb[0:1, 2 * W:3 * W]
                P.op("vector", lambda e: e.tensor_scalar_mul(mrow, p5[0:1, :], 1.0 / D),
                     waits=wst + wrows, inc=s_dve)
                P.op("vector", lambda e: e.tensor_scalar_mul(vrow, p6[0:1, :], 1.0 / D),
                     inc=s_dve)
                bmark(5, s_dve)
                bmark(6, s_dve)
                P.op("vector", lambda e: e.tensor_mul(rrow, mrow, mrow),
                     waits=P.now(s_dve), inc=s_dve)
                P.op("vector", lambda e: e.tensor_sub(vrow, vrow, rrow),
                     waits=P.now(s_dve), inc=s_dve)
                P.op("scalar", lambda e: e.activation(rrow, vrow, AF.Ln,
                                                      bias=EPS, scale=1.0),
                     waits=P.now(s_dve), inc=s_act)
                P.op("scalar", lambda e: e.activation(rrow, rrow, AF.Exp,
                                                      bias=0.0, scale=-0.5),
                     waits=P.now(s_act), inc=s_act)
                mbf = stbf_sb[0:1, 0:W]
                rbf = stbf_sb[0:1, W:2 * W]
                wr2 = P.now(s_act)
                P.op("vector", lambda e: e.tensor_copy(mbf, mrow), waits=wr2, inc=s_dve)
                P.op("vector", lambda e: e.tensor_copy(rbf, rrow), inc=s_dve)
                wb_ = P.now(s_dve)
                P.op("tensor", lambda e: e.matmul(p5[:, :], ones_row_bf[0:1, :], mbf,
                                                  start=True, stop=True),
                     waits=wb_ + bfree(5), inc=s_pe)
                P.op("tensor", lambda e: e.matmul(p6[:, :], ones_row_bf[0:1, :], rbf,
                                                  start=True, stop=True),
                     waits=bfree(6), inc=s_pe)
                wbc = P.now(s_pe)
                for kc in range(KC):
                    P.op("vector", lambda e, kc=kc: e.tensor_sub(
                        xbf_sb[:, kc * W:(kc + 1) * W], src_slices(kc), p5[:, :]),
                        waits=wbc if kc == 0 else (), inc=s_dve)
                    P.op("vector", lambda e, kc=kc: e.tensor_mul(
                        out_sb[:, kc * W:(kc + 1) * W],
                        xbf_sb[:, kc * W:(kc + 1) * W], p6[:, :]),
                        waits=P.now(s_dve), inc=s_dve)
                bmark(5, s_dve)
                bmark(6, s_dve)
                return P.now(s_dve)

            def emit_proj_fm(wsb, bias_sb, out_sb, wln, scale):
                for mc in range(KC):
                    pb = PB[mc % 2]
                    wb = bfree(mc % 2)
                    for kcc in range(KC):
                        P.op("tensor", lambda e, mc=mc, kcc=kcc, pb=pb: e.matmul(
                            pb[:, :],
                            wsb[:, kcc * D + mc * 128: kcc * D + (mc + 1) * 128],
                            xln_sb[:, kcc * W:(kcc + 1) * W],
                            start=(kcc == 0), stop=(kcc == KC - 1)),
                            waits=(wln + wb + attw) if kcc == 0 else (), inc=s_pe)
                    wmm = P.now(s_pe)
                    P.op("scalar", lambda e, mc=mc, pb=pb: e.activation(
                        out_sb[:, mc * W:(mc + 1) * W], pb[:, :], AF.Identity,
                        bias=bias_sb[:, mc:mc + 1], scale=scale),
                        waits=wmm, inc=s_act)
                    bmark(mc % 2, s_act)
                return P.now(s_act)

            def emit_proj_v(vr, wln):
                base = vr * 4 * VROW
                for tc in range(4):
                    for half in range(2):
                        bi = (tc * 2 + half) % 2
                        pb = PB[bi]
                        wb = bfree(bi)
                        for kcc in range(KC):
                            P.op("tensor", lambda e, tc=tc, half=half, kcc=kcc, pb=pb: e.matmul(
                                pb[:, :],
                                xln_sb[:, kcc * W + tc * 128: kcc * W + (tc + 1) * 128],
                                wv_sb[:, kcc * D + half * 512: kcc * D + (half + 1) * 512],
                                start=(kcc == 0), stop=(kcc == KC - 1)),
                                waits=(wln + wb + attw) if kcc == 0 else (), inc=s_pe)
                        wmm = P.now(s_pe)
                        off = base + tc * VROW + half * 8 * 65
                        dst = v_sb[:, off: off + 8 * 65].rearrange(
                            "p (s d) -> p s d", d=65)[:, :, 0:64]
                        P.op("vector", lambda e, pb=pb, dst=dst: e.tensor_copy(
                            dst, pb[:, :].rearrange("p (s d) -> p s d", d=64)),
                            waits=wmm, inc=s_dve)
                        bmark(bi, s_dve)
                return P.now(s_dve)

            def emit_attention(c, negc_ready):
                rings = [c % 2] if c == 0 else [(c - 1) % 2, c % 2]
                nsub = 4 * len(rings)
                wqkv = P.now(s_act, s_dve)
                exp_done = {}
                av_done = {}

                def escore(h):
                    nonlocal negc_ready
                    hb = (h % 2) * 64
                    hc = h // 2
                    aoff = (h % 2) * KC * W
                    for j in range(nsub):
                        ring = rings[j // 4]
                        tj = j % 4
                        bi = 1 + (h * nsub + j) % 3
                        pb = PB[bi]
                        wb = bfree(bi)
                        P.op("tensor", lambda e, ring=ring, tj=tj, pb=pb, hb=hb, hc=hc: e.matmul(
                            pb[:, :],
                            k_sb[hb:hb + 64, ring * KC * W + hc * W + tj * 128:
                                 ring * KC * W + hc * W + (tj + 1) * 128],
                            q_sb[hb:hb + 64, hc * W:(hc + 1) * W],
                            start=True, stop=True),
                            waits=wqkv + wb, inc=s_pe)
                        wmm = P.now(s_pe)
                        if negc_ready is None:
                            P.op("vector", lambda e, pb=pb: e.reduce_max(
                                sm_sb[:, 0:1], pb[:, :], axis=AX.X),
                                waits=wmm, inc=s_dve)
                            P.op("tensor", lambda e: e.transpose(
                                p5[0:1, 0:128], sm_sb[:, 0:1], ident_f[:, :]),
                                waits=P.now(s_dve) + bfree(5), inc=s_pe)
                            P.op("vector", lambda e: e.reduce_max(
                                sm_sb[0:1, 1:2], p5[0:1, 0:128], axis=AX.X),
                                waits=P.now(s_pe), inc=s_dve)
                            bmark(5, s_dve)
                            P.op("vector", lambda e: e.tensor_scalar_mul(
                                sm_sb[0:1, 1:2], sm_sb[0:1, 1:2], -1.0),
                                waits=P.now(s_dve), inc=s_dve)
                            P.op("vector", lambda e: e.tensor_copy(
                                stbf_sb[0:1, 2 * W:2 * W + 1], sm_sb[0:1, 1:2]),
                                waits=P.now(s_dve), inc=s_dve)
                            P.op("tensor", lambda e: e.matmul(
                                p5[:, 0:1], ones_row_bf[0:1, :],
                                stbf_sb[0:1, 2 * W:2 * W + 1],
                                start=True, stop=True),
                                waits=P.now(s_dve) + bfree(5), inc=s_pe)
                            P.op("vector", lambda e: e.tensor_copy(
                                sm_sb[:, 2:3], p5[:, 0:1]),
                                waits=P.now(s_pe), inc=s_dve)
                            bmark(5, s_dve)
                            negc_ready = P.now(s_dve)
                        P.op("scalar", lambda e, j=j, pb=pb, aoff=aoff: e.activation(
                            a_sb[:, aoff + j * W: aoff + (j + 1) * W], pb[:, :],
                            AF.Exp, bias=sm_sb[:, 2:3], scale=1.0),
                            waits=P.now(s_pe) + negc_ready, inc=s_act)
                        bmark(bi, s_act)
                    exp_done[h] = P.now(s_act)

                def eav(h):
                    bi = 4 + h % 3
                    pbav = PB[bi]
                    wavb = bfree(bi)
                    aoff = (h % 2) * KC * W
                    for j in range(nsub):
                        ring = rings[j // 4]
                        tj = j % 4
                        voff = ring * 4 * VROW + tj * VROW + h * 65
                        P.op("tensor", lambda e, j=j, voff=voff, pbav=pbav, aoff=aoff:
                             e.matmul(
                                 pbav[0:65, :], v_sb[:, voff: voff + 65],
                                 a_sb[:, aoff + j * W: aoff + (j + 1) * W],
                                 start=(j == 0), stop=(j == nsub - 1)),
                             waits=(exp_done[h] + wavb + attw) if j == 0 else (),
                             inc=s_pe)
                    av_done[h] = P.now(s_pe)

                def etail(h):
                    bi = 4 + h % 3
                    pbav = PB[bi]
                    par = h % 2
                    lr = st_sb[0:1, par * W:(par + 1) * W]
                    er = stbf_sb[0:1, par * W:(par + 1) * W]
                    P.op("scalar", lambda e, pbav=pbav, lr=lr: e.activation(
                        lr, pbav[64:65, :], AF.Ln, bias=0.0, scale=1.0),
                        waits=av_done[h], inc=s_act)
                    P.op("scalar", lambda e, lr=lr, er=er: e.activation(
                        er, lr, AF.Exp, bias=0.0, scale=-1.0),
                        waits=P.now(s_act), inc=s_act)
                    P.op("tensor", lambda e, er=er: e.matmul(
                        p0[0:64, :], ones_row_bf[0:1, 0:64], er,
                        start=True, stop=True),
                        waits=P.now(s_act) + bfree(0), inc=s_pe)
                    P.op("vector", lambda e: e.tensor_copy(rbc_sb[:, :], p0[0:64, :]),
                         waits=P.now(s_pe), inc=s_dve)
                    bmark(0, s_dve)
                    P.op("vector", lambda e, h=h, pbav=pbav: e.tensor_mul(
                        o_sb[0:64, h * W:(h + 1) * W], pbav[0:64, :], rbc_sb[:, :]),
                        waits=P.now(s_dve), inc=s_dve)
                    P.op("vector", lambda e, h=h: e.tensor_scalar_add(
                        o_sb[0:64, h * W:(h + 1) * W],
                        o_sb[0:64, h * W:(h + 1) * W], bvc_sb[0:64, h:h + 1]),
                        waits=P.now(s_dve), inc=s_dve)
                    bmark(bi, s_dve)

                escore(0)
                escore(1)
                eav(0)
                for h in range(2, NH):
                    escore(h)
                    eav(h - 1)
                    etail(h - 2)
                qfree = P.now(s_pe)
                eav(NH - 1)
                etail(NH - 2)
                etail(NH - 1)
                return P.now(s_dve), negc_ready, qfree

            wos_marks = [(), ()]

            def emit_outproj_residual(ring, watt):
                wslab = {}
                for mc in range(2):
                    P.dma("sync", wos_sb[:, (mc % 2) * NH * 128:(mc % 2 + 1) * NH * 128]
                          .rearrange("d (h n) -> d h n", h=NH), wobf[mc],
                          s_x, waits=list(wos_marks[mc % 2]) + list(wocast_mark))
                    wslab[mc] = P.now(s_x)
                for mc in range(KC):
                    bi = mc % 2
                    pb = PB[bi]
                    wb = bfree(bi)
                    for h in range(NH):
                        P.op("tensor", lambda e, mc=mc, h=h, pb=pb: e.matmul(
                            pb[:, :],
                            wos_sb[0:64, (mc % 2) * NH * 128 + h * 128:
                                   (mc % 2) * NH * 128 + (h + 1) * 128],
                            o_sb[0:64, h * W:(h + 1) * W],
                            start=(h == 0), stop=(h == NH - 1)),
                            waits=(watt + wb + wslab[mc]) if h == 0 else (), inc=s_pe)
                    wos_marks[mc % 2] = P.now(s_pe)
                    if mc + 2 < KC:
                        P.dma("sync", wos_sb[:, (mc % 2) * NH * 128:(mc % 2 + 1) * NH * 128]
                              .rearrange("d (h n) -> d h n", h=NH), wobf[mc + 2],
                              s_x, waits=list(wos_marks[mc % 2]))
                        wslab[mc + 2] = P.now(s_x)
                    wmm = P.now(s_pe)
                    P.op("vector", lambda e, mc=mc, pb=pb, ring=ring: e.scalar_tensor_tensor(
                        xsl(ring, mc), pb[:, :], bo_sb[:, mc:mc + 1],
                        xsl(ring, mc), OP.add, OP.add), waits=wmm, inc=s_dve)
                    bmark(bi, s_dve)
                return P.now(s_dve)

            def emit_router_logits(s, c, wln2):
                wb = bfree(5)
                for tc in range(4):
                    for kcc in range(KC):
                        P.op("tensor", lambda e, tc=tc, kcc=kcc: e.matmul(
                            p5[:, tc * NE:(tc + 1) * NE],
                            ln2_sb[:, kcc * W + tc * 128: kcc * W + (tc + 1) * 128],
                            wr_sb[:, kcc * NE:(kcc + 1) * NE],
                            start=(kcc == 0), stop=(kcc == KC - 1)),
                            waits=(wln2 + wb + attw) if (tc == 0 and kcc == 0) else (),
                            inc=s_pe)
                wmm = P.now(s_pe)
                g0 = s * GSEQ + c * 4
                wg0 = wginit if (s == 0 and c == 0) else []
                P.op("vector", lambda e, g0=g0: e.tensor_copy(
                    logits_sb[:, g0 * NE:(g0 + 4) * NE], p5[:, 0:4 * NE]),
                    waits=wmm + list(wg0), inc=s_dve)
                bmark(5, s_dve)

            lt2_marks = [(), ()]

            def emit_ln2_tok_store(s, c, wln2):
                for tc in range(4):
                    lr = tc % 2
                    wb = bfree(7)
                    for kcc in range(KC):
                        P.op("tensor", lambda e, tc=tc, kcc=kcc: e.transpose(
                            p7[:, kcc * 128:(kcc + 1) * 128],
                            ln2_sb[:, kcc * W + tc * 128: kcc * W + (tc + 1) * 128],
                            ident_bf[:, :]),
                            waits=(wln2 + wb) if kcc == 0 else (), inc=s_pe)
                    wtr = P.now(s_pe)
                    P.op("vector", lambda e, lr=lr: e.tensor_copy(
                        lt2_sb[:, lr * 1024:(lr + 1) * 1024], p7[:, :]),
                        waits=wtr + list(lt2_marks[lr]), inc=s_dve)
                    bmark(7, s_dve)
                    r0 = s * L + c * W + tc * 128
                    P.dma("sync", ln2tok_d[r0:r0 + 128, :],
                          lt2_sb[:, lr * 1024:(lr + 1) * 1024], s_st,
                          waits=P.now(s_dve))
                    lt2_marks[lr] = P.now(s_st)

            RT_D, RT_EQ, RT_MK, RT_SEL, RT_S = 0, 128, 256, 384, 512
            RT_M1, RT_M2D, RT_W1, RT_W2 = 640, 656, 672, 688

            def rt(a, n=128):
                return rt_sb[:, a:a + n]

            def rt3(a, n=128):
                return rt_sb[:, a:a + n].rearrange("p (g e) -> p g e", e=NE)

            def rtb(a):
                return rt_sb[:, a:a + 16].rearrange("p (g one) -> p g one", one=1) \
                    .broadcast_to([128, 16, NE])

            def emit_routing_masks(s):
                g0 = s * GSEQ
                l3 = logits_sb[:, g0 * NE:(g0 + GSEQ) * NE].rearrange(
                    "p (g e) -> p g e", e=NE)
                P.op("vector", lambda e, l3=l3: e.tensor_reduce(
                    rt(RT_M1, 16).rearrange("p (g one) -> p g one", one=1),
                    l3, AX.X, OP.max), waits=P.now(s_pe), inc=s_dve)
                P.op("vector", lambda e, l3=l3: e.tensor_sub(
                    rt3(RT_D), l3, rtb(RT_M1)), waits=P.now(s_dve), inc=s_dve)
                P.op("vector", lambda e: e.tensor_single_scalar(
                    rt(RT_EQ), rt(RT_D), 0.0, OP.is_ge),
                    waits=P.now(s_dve), inc=s_dve)
                P.op("vector", lambda e: e.scalar_tensor_tensor(
                    rt(RT_MK), rt(RT_EQ), -1e9, rt(RT_D), OP.mult, OP.add),
                    waits=P.now(s_dve), inc=s_dve)
                P.op("vector", lambda e: e.tensor_reduce(
                    rt(RT_M2D, 16).rearrange("p (g one) -> p g one", one=1),
                    rt3(RT_MK), AX.X, OP.max), waits=P.now(s_dve), inc=s_dve)
                P.op("scalar", lambda e: e.activation(
                    rt(RT_W1, 16), rt(RT_M2D, 16), AF.Exp, bias=0.0, scale=1.0),
                    waits=P.now(s_dve), inc=s_act)
                P.op("vector", lambda e: e.tensor_scalar_add(
                    rt(RT_W1, 16), rt(RT_W1, 16), 1.0), waits=P.now(s_act), inc=s_dve)
                P.op("vector", lambda e: e.reciprocal(
                    rt(RT_W1, 16), rt(RT_W1, 16)), waits=P.now(s_dve), inc=s_dve)
                P.op("vector", lambda e: e.tensor_scalar(
                    rt(RT_W2, 16), rt(RT_W1, 16), -1.0, 1.0, OP.mult, OP.add),
                    waits=P.now(s_dve), inc=s_dve)
                P.op("vector", lambda e: e.tensor_tensor(
                    rt3(RT_SEL), rt3(RT_D), rtb(RT_M2D), OP.is_ge),
                    waits=P.now(s_dve), inc=s_dve)
                P.op("vector", lambda e: e.tensor_sub(
                    rt(RT_S), rt(RT_SEL), rt(RT_EQ)), waits=P.now(s_dve), inc=s_dve)
                P.op("vector", lambda e, s=s: e.tensor_copy(
                    wv12_sb[:, s * 16:(s + 1) * 16], rt(RT_W1, 16)),
                    waits=P.now(s_dve), inc=s_dve)
                P.op("vector", lambda e, s=s: e.tensor_copy(
                    wv12_sb[:, NG + s * 16: NG + (s + 1) * 16], rt(RT_W2, 16)),
                    inc=s_dve)
                P.op("vector", lambda e: e.tensor_copy(
                    fe_all[:, :].rearrange("p (e g) -> p e g", g=GSEQ),
                    rt3(RT_SEL).transpose([0, 2, 1])),
                    waits=P.now(s_dve) + P.now(s_pe), inc=s_dve)

            def emit_routing_pos(s):
                P.op("tensor", lambda e: e.transpose(
                    p7[:, 0:128], fe_all[:, :], ident_bf[:, :]),
                    waits=P.now(s_dve) + bfree(7), inc=s_pe)
                P.op("vector", lambda e: e.tensor_copy(
                    ft_all[:, :], p7[:, 0:128]), waits=P.now(s_pe), inc=s_dve)
                bmark(7, s_dve)
                P.op("tensor", lambda e: e.matmul(
                    p5[:, 0:128], ltbd_sb[:, :], ft_all[:, :],
                    start=True, stop=True),
                    waits=P.now(s_dve) + bfree(5), inc=s_pe)
                P.op("vector", lambda e: e.tensor_copy(
                    ct_all[:, :], p5[:, 0:128]), waits=P.now(s_pe), inc=s_dve)
                bmark(5, s_dve)
                P.op("tensor", lambda e: e.transpose(
                    p6[:, 0:128], ct_all[:, :], ident_f[:, :]),
                    waits=P.now(s_dve) + bfree(6), inc=s_pe)
                P.op("vector", lambda e: e.tensor_copy(
                    pm_all[:, :], p6[:, 0:128]), waits=P.now(s_pe), inc=s_dve)
                bmark(6, s_dve)
                P.op("vector", lambda e: e.tensor_reduce(
                    tot_sb[:, :].rearrange("p (e one) -> p e one", one=1),
                    rt3(RT_SEL).transpose([0, 2, 1]), AX.X, OP.add),
                    waits=P.now(s_dve), inc=s_dve)
                P.op("tensor", lambda e: e.matmul(
                    p5[:, 0:NE], ptri_sb[:, :], tot_sb[:, :],
                    start=True, stop=True),
                    waits=P.now(s_dve) + bfree(5), inc=s_pe)
                P.op("vector", lambda e: e.tensor_copy(
                    bcol8_sb[:, :], p5[:, 0:NE]), waits=P.now(s_pe), inc=s_dve)
                bmark(5, s_dve)
                P.op("vector", lambda e: e.tensor_add(
                    posall_sb[:, :].rearrange("p (g e) -> p g e", e=NE),
                    pm_all[:, :].rearrange("p (e g) -> p e g", g=GSEQ)
                    .transpose([0, 2, 1]),
                    bcol8_sb[:, :].rearrange("p (one e) -> p one e", one=1)
                    .broadcast_to([128, GSEQ, NE])),
                    waits=P.now(s_dve), inc=s_dve)
                pos3 = posall_sb[:, :].rearrange("p (g e) -> p g e", e=NE)
                eb = evals_sb[:, :].rearrange("p (one e) -> p one e", one=1) \
                    .broadcast_to([128, 16, NE])
                SC = RT_D
                for i, (msk, dst) in enumerate(
                        ((RT_EQ, SC), (RT_EQ, SC + 16), (RT_S, SC + 32),
                         (RT_S, SC + 48))):
                    src_b = eb if i % 2 == 0 else pos3
                    P.op("vector", lambda e, msk=msk, src_b=src_b: e.tensor_mul(
                        rt3(RT_MK), rt3(msk), src_b),
                        waits=P.now(s_dve), inc=s_dve)
                    P.op("vector", lambda e, dst=dst: e.tensor_reduce(
                        rt(dst, 16).rearrange("p (g one) -> p g one", one=1),
                        rt3(RT_MK), AX.X, OP.add), waits=P.now(s_dve), inc=s_dve)
                for dst in (SC + 16, SC + 48):
                    P.op("vector", lambda e, dst=dst: e.tensor_scalar_min(
                        rt(dst, 16), rt(dst, 16), float(CAPS - 1)),
                        waits=P.now(s_dve), inc=s_dve)
                for ei, pi, oi in ((SC, SC + 16, 0), (SC + 32, SC + 48, 16)):
                    P.op("vector", lambda e, ei=ei, s=s: e.tensor_scalar(
                        rt(ei, 16), rt(ei, 16), float(n_seq), float(s),
                        OP.mult, OP.add), waits=P.now(s_dve), inc=s_dve)
                    P.op("vector", lambda e, ei=ei: e.tensor_scalar_mul(
                        rt(ei, 16), rt(ei, 16), float(CAPS)),
                        waits=P.now(s_dve), inc=s_dve)
                    P.op("vector", lambda e, ei=ei, pi=pi: e.tensor_add(
                        rt(ei, 16), rt(ei, 16), rt(pi, 16)),
                        waits=P.now(s_dve), inc=s_dve)
                    P.op("vector", lambda e, ei=ei, oi=oi: e.tensor_copy(
                        offi_sb[:, oi:oi + 16], rt(ei, 16)),
                        waits=P.now(s_dve) + P.now(s_w), inc=s_dve)
                P.op("vector", lambda e, s=s: e.tensor_scalar_add(
                    valb_sb[:, :], ids_sb[:, s * 16:(s + 1) * 16], 8192.0),
                    waits=P.now(s_dve) + P.now(s_w), inc=s_dve)
                woff = P.now(s_dve)
                for g in range(16):
                    P.op("gpsimd", lambda e, g=g, s=s: e.indirect_dma_start(
                        g_d, IndirectOffsetOnAxis(ap=offi_sb[:, g:g + 1], axis=0),
                        ids_sb[:, s * 16 + g: s * 16 + g + 1], None),
                        waits=(woff + P.now(s_st)) if g == 0 else (),
                        inc=s_w, inc_n=16)
                    P.op("gpsimd", lambda e, g=g: e.indirect_dma_start(
                        g_d, IndirectOffsetOnAxis(ap=offi_sb[:, 16 + g:17 + g], axis=0),
                        valb_sb[:, g:g + 1], None),
                        inc=s_w, inc_n=16)

            wln = None
            for s in range(n_seq):
                negc_ready = None
                for c in range(NCH):
                    ring = (s * NCH + c) % 2
                    nxt = (s, c + 1) if c + 1 < NCH else (
                        (s + 1, 0) if s + 1 < n_seq else None)
                    if wln is None:
                        loads_cur = loads.pop((s, c))
                        wln = emit_ln(lambda kc, ring=ring: xsl(ring, kc), xln_sb,
                                      loads_cur + P.now(s_pe, s_act, s_dve, s_st))
                    kr = c % 2
                    for mc in range(KC):
                        pb = PB[mc % 2]
                        wb = bfree(mc % 2)
                        for kcc in range(KC):
                            P.op("tensor", lambda e, mc=mc, kcc=kcc, pb=pb: e.matmul(
                                pb[:, :],
                                wk_sb[:, kcc * D + mc * 128: kcc * D + (mc + 1) * 128],
                                xln_sb[:, kcc * W:(kcc + 1) * W],
                                start=(kcc == 0), stop=(kcc == KC - 1)),
                                waits=(wln + wb + attw + P.now(s_pe)) if kcc == 0 else (),
                                inc=s_pe)
                        wmm = P.now(s_pe)
                        P.op("scalar", lambda e, mc=mc, pb=pb, kr=kr: e.activation(
                            k_sb[:, kr * KC * W + mc * W: kr * KC * W + (mc + 1) * W],
                            pb[:, :], AF.Identity, bias=bk_sb[:, mc:mc + 1], scale=1.0),
                            waits=wmm, inc=s_act)
                        bmark(mc % 2, s_act)
                    emit_proj_v(kr, wln)
                    emit_proj_fm(wq_sb, bq_sb, q_sb, wln + P.now(s_dve), QSCALE)
                    watt, negc_ready, qfree = emit_attention(c, negc_ready)
                    if c == 0 and s > 0:
                        emit_routing_pos(s - 1)
                    wmid = emit_outproj_residual(ring, watt)
                    P.dma("sync", xmid_d[s, :, c * W:(c + 1) * W]
                          .rearrange("(kc p) n -> p kc n", p=128),
                          x_sb[:, ring * KC * W:(ring + 1) * KC * W]
                          .rearrange("p (kc n) -> p kc n", kc=KC), s_st, waits=wmid)
                    st_mark = P.now(s_st)
                    wln2 = emit_ln(lambda kc, ring=ring: xsl(ring, kc), ln2_sb,
                                   wmid + P.now(s_pe, s_act))
                    xring_free[ring] = wln2 + st_mark
                    emit_router_logits(s, c, wln2)
                    emit_ln2_tok_store(s, c, wln2)
                    if nxt is not None:
                        loads[nxt] = emit_xload(*nxt)
                        rn = (nxt[0] * NCH + nxt[1]) % 2
                        wln = emit_ln(lambda kc, rn=rn: xsl(rn, kc), xln_sb,
                                      loads[nxt] + qfree)
                    else:
                        wln = None
                    if c == NCH - 1:
                        emit_routing_masks(s)
            emit_routing_pos(n_seq - 1)

            if dev_taps:
                wall = P.now(s_dve, s_act, s_pe)
                P.dma("sync", taps["t_logits"], logits_sb[:, 0:ng * NE], s_out,
                      waits=wall)
        ph = P.now(s_pe, s_act, s_dve, s_st, s_x, s_w)

        cap_e = n_seq * CAPS
        nblk = cap_e // 128
        with contextlib.ExitStack() as _fs:
            fent = _fs.enter_context
            w1t_sb = fent(nc.sbuf_tensor("w1t_sb", [128, KC * DFF], BF16))
            w2t_sb = fent(nc.sbuf_tensor("w2t_sb", [128, 32 * D], BF16))
            h_sb = fent(nc.sbuf_tensor("h_sb", [128, 32 * 512], BF16))
            xg_sb = fent(nc.sbuf_tensor("xg_sb", [128, 2 * 4 * D], BF16))
            xct_sb = fent(nc.sbuf_tensor("xct_sb", [128, KC * 512], BF16))
            ys_sb = fent(nc.sbuf_tensor("ys_sb", [128, 2 * D], BF16))
            b2r_sb = fent(nc.sbuf_tensor("b2r_sb", [1, 2 * D], BF16))
            gf_sb = fent(nc.sbuf_tensor("gf_sb", [nblk, 128], F32))
            gcol_sb = fent(nc.sbuf_tensor("gcol_sb", [128, nblk], F32))
            gi_sb = fent(nc.sbuf_tensor("gi_sb", [128, 2 * nblk], I32))
            gtok_sb = fent(nc.sbuf_tensor("gtok_sb", [128, 2 * nblk], I32))
            gsc_sb = fent(nc.sbuf_tensor("gsc_sb", [128, 2 * nblk], I32))
            gt2_sb = fent(nc.sbuf_tensor("gt2_sb", [128, nblk], I32))
            zz_sb = fent(nc.sbuf_tensor("zz_sb", [128, D], BF16))

            P.op("vector", lambda e: e.memset(zz_sb[:, :], 0.0), waits=ph,
                 inc=s_dve)
            wz = P.now(s_dve)
            nsl = (2 * tok + 128) // 128
            for b in range(nsl):
                P.dma("sync", slot_d[b * 128:(b + 1) * 128, :], zz_sb[:, :],
                      s_st, waits=wz if b == 0 else ())
            wslz = P.now(s_st)

            NCC = (nblk + 3) // 4
            h_mark = [()]

            def emit_wload(eidx, which, waits):
                if which == "w1lo":
                    P.dma("sync", w1t_sb[:, :].rearrange("p (kc m) -> p kc m", kc=KC)
                          [:, :, 0:DFF // 2],
                          w1bf[eidx].rearrange("(kc p) m -> p kc m", p=128)
                          [:, :, 0:DFF // 2], s_wf, waits=waits)
                elif which == "w1hi":
                    P.dma("sync", w1t_sb[:, :].rearrange("p (kc m) -> p kc m", kc=KC)
                          [:, :, DFF // 2:DFF],
                          w1bf[eidx].rearrange("(kc p) m -> p kc m", p=128)
                          [:, :, DFF // 2:DFF], s_wf, waits=waits)
                else:
                    P.dma("sync", w2t_sb[:, :].rearrange("p (kc m) -> p kc m", kc=32),
                          w2bf[eidx].rearrange("(kc p) m -> p kc m", p=128), s_wf,
                          waits=waits)
                return P.now(s_wf)

            def emit_gprep(eidx, gpar, waits):
                gofs = gpar * nblk
                P.dma("sync", gf_sb[:, :],
                      g_d[eidx * cap_e:(eidx + 1) * cap_e, 0:1]
                      .rearrange("(r n) one -> r (n one)", n=128), s_x,
                      waits=waits)
                P.op("tensor", lambda e: e.transpose(
                    p6[0:128, 0:nblk], gf_sb[:, :], ident_f[0:nblk, 0:nblk]),
                    waits=P.now(s_x) + bfree(6), inc=s_pe)
                P.op("vector", lambda e: e.tensor_copy(
                    gcol_sb[:, :], p6[0:128, 0:nblk]), waits=P.now(s_pe), inc=s_dve)
                bmark(6, s_dve)
                sl = slice(gofs, gofs + nblk)
                P.op("vector", lambda e, sl=sl: e.tensor_copy(
                    gi_sb[:, sl], gcol_sb[:, :]), waits=P.now(s_dve), inc=s_dve)
                P.op("vector", lambda e, sl=sl: e.tensor_scalar(
                    gtok_sb[:, sl], gi_sb[:, sl], 8191, None, OP.bitwise_and),
                    waits=P.now(s_dve), inc=s_dve)
                P.op("vector", lambda e, sl=sl: e.tensor_scalar(
                    gsc_sb[:, sl], gi_sb[:, sl], 13, 1, OP.logical_shift_right,
                    OP.bitwise_and), waits=P.now(s_dve), inc=s_dve)
                P.op("vector", lambda e, sl=sl: e.tensor_scalar(
                    gt2_sb[:, :], gtok_sb[:, sl], 2, None, OP.mult),
                    waits=P.now(s_dve), inc=s_dve)
                P.op("vector", lambda e, sl=sl: e.tensor_add(
                    gsc_sb[:, sl], gsc_sb[:, sl], gt2_sb[:, :]),
                    waits=P.now(s_dve), inc=s_dve)
                P.op("vector", lambda e, sl=sl: e.tensor_scalar(
                    gt2_sb[:, :], gi_sb[:, sl], 20, None, OP.logical_shift_right),
                    waits=P.now(s_dve), inc=s_dve)
                P.op("vector", lambda e: e.tensor_scalar(
                    gt2_sb[:, :], gt2_sb[:, :], 1 << 18, None, OP.mult),
                    waits=P.now(s_dve), inc=s_dve)
                P.op("vector", lambda e, sl=sl: e.tensor_add(
                    gsc_sb[:, sl], gsc_sb[:, sl], gt2_sb[:, :]),
                    waits=P.now(s_dve), inc=s_dve)
                P.op("vector", lambda e, sl=sl: e.tensor_scalar_min(
                    gsc_sb[:, sl], gsc_sb[:, sl], 2 * tok),
                    waits=P.now(s_dve), inc=s_dve)
                return P.now(s_dve)

            wfr_lo = {0: emit_wload(0, "w1lo", ph + cast_marks[0])}
            wfr_hi = {0: emit_wload(0, "w1hi", [])}
            wfr_w2 = {0: emit_wload(0, "w2", [])}
            widx_mark = {0: emit_gprep(0, 0, ph)}
            P.dma("gpsimd", b2r_sb[0:1, 0:D],
                  b2r[0].rearrange("(one d) -> one d", one=1), s_w, waits=ph)
            wb2r = {0: P.now(s_w)}
            g_mark = {}
            xg_tmark = [[], []]
            NJ = NE * NCC

            def emit_gather(j):
                ej = j // NCC
                ccj = j % NCC
                gofs_j = (ej % 2) * nblk
                xgo = (j % 2) * 4 * D
                for b in range(4):
                    blk_i = ccj * 4 + b
                    P.op("gpsimd", lambda e, b=b, blk_i=blk_i, xgo=xgo, gofs_j=gofs_j:
                         e.indirect_dma_start(
                             xg_sb[:, xgo + b * D: xgo + (b + 1) * D], None,
                             ln2tok_d,
                             IndirectOffsetOnAxis(
                                 ap=gtok_sb[:, gofs_j + blk_i: gofs_j + blk_i + 1],
                                 axis=0)),
                         waits=(widx_mark[ej] + list(xg_tmark[j % 2]))
                         if b == 0 else (), inc=s_g, inc_n=16)
                g_mark[j] = P.now(s_g)

            def emit_T(j):
                xgo = (j % 2) * 4 * D
                for b in range(4):
                    wb7 = bfree(7)
                    for kcc in range(KC):
                        P.op("tensor", lambda e, b=b, kcc=kcc, xgo=xgo: e.transpose(
                            p7[:, kcc * 128:(kcc + 1) * 128],
                            xg_sb[:, xgo + b * D + kcc * 128:
                                  xgo + b * D + (kcc + 1) * 128],
                            ident_bf[:, :]),
                            waits=(g_mark[j] + wb7) if kcc == 0 else (), inc=s_pe)
                    wtr = P.now(s_pe)
                    dst = xct_sb[:, :].rearrange(
                        "p (kc n) -> p kc n", kc=KC)[:, :, b * 128:(b + 1) * 128]
                    P.op("vector", lambda e, dst=dst: e.tensor_copy(
                        dst, p7[:, :].rearrange("p (kc n) -> p kc n", kc=KC)),
                        waits=wtr, inc=s_dve)
                    bmark(7, s_dve)
                xg_tmark[j % 2] = P.now(s_pe)
                return P.now(s_dve)

            emit_gather(0)
            emit_gather(1)
            wxc = emit_T(0)
            first_scatter = True

            for j in range(NJ):
                eidx = j // NCC
                cc = j % NCC
                gofs = (eidx % 2) * nblk
                if j + 2 < NJ:
                    emit_gather(j + 2)
                for m in range(32):
                    bi = m % 2
                    pb = PB[bi]
                    wb = bfree(bi)
                    w0 = []
                    if m == 0:
                        w0 = wxc + wfr_lo[eidx] + list(h_mark[0])
                    elif m == 16:
                        w0 = list(wfr_hi[eidx])
                    for kcc in range(KC):
                        P.op("tensor", lambda e, m=m, kcc=kcc, pb=pb: e.matmul(
                            pb[:, :],
                            w1t_sb[:, kcc * DFF + m * 128: kcc * DFF + (m + 1) * 128],
                            xct_sb[:, kcc * 512:(kcc + 1) * 512],
                            start=(kcc == 0), stop=(kcc == KC - 1)),
                            waits=(wb + w0) if kcc == 0 else (), inc=s_pe)
                    wmm = P.now(s_pe)
                    P.op("scalar", lambda e, m=m, pb=pb, eidx=eidx: e.activation(
                        h_sb[:, m * 512:(m + 1) * 512], pb[:, :], AF.Gelu,
                        bias=b1_sb[:, eidx * 32 + m: eidx * 32 + m + 1],
                        scale=1.0), waits=wmm, inc=s_act)
                    bmark(bi, s_act)
                    if cc == NCC - 1 and eidx + 1 < NE:
                        if m == 15:
                            wfr_lo[eidx + 1] = emit_wload(
                                eidx + 1, "w1lo", P.now(s_pe) + cast_marks[eidx + 1])
                        elif m == 31:
                            wfr_hi[eidx + 1] = emit_wload(
                                eidx + 1, "w1hi", P.now(s_pe))
                if cc == NCC - 3 and eidx + 1 < NE:
                    widx_mark[eidx + 1] = emit_gprep(
                        eidx + 1, 1 - eidx % 2, P.now(s_g, s_w, s_pe))
                if cc == NCC - 2 and eidx + 1 < NE:
                    po = ((eidx + 1) % 2) * D
                    P.dma("gpsimd", b2r_sb[0:1, po:po + D],
                          b2r[eidx + 1].rearrange("(one d) -> one d", one=1), s_w,
                          waits=P.now(s_pe))
                    wb2r[eidx + 1] = P.now(s_w)
                wxc_next = emit_T(j + 1) if j + 1 < NJ else []
                wh = P.now(s_act)
                for mp in range(4):
                    blk_i = cc * 4 + mp
                    bA = 2 + (mp % 2) * 2
                    wbA = bfree(bA) + bfree(bA + 1)
                    for m in range(32):
                        for half in range(2):
                            pb = PB[bA + half]
                            w0 = []
                            if m == 0 and half == 0:
                                w0 = wh + wbA
                                if mp == 0:
                                    w0 = w0 + list(wfr_w2[eidx])
                            P.op("tensor", lambda e, m=m, mp=mp, half=half, pb=pb:
                                 e.matmul(
                                     pb[:, :],
                                     h_sb[:, m * 512 + mp * 128: m * 512 + (mp + 1) * 128],
                                     w2t_sb[:, m * D + half * 512: m * D + (half + 1) * 512],
                                     start=(m == 0), stop=False),
                                 waits=w0, inc=s_pe)
                    for half in range(2):
                        pb = PB[bA + half]
                        po = (eidx % 2) * D
                        P.op("tensor", lambda e, half=half, pb=pb, po=po: e.matmul(
                            pb[:, :], ones_row_bf[0:1, :],
                            b2r_sb[0:1, po + half * 512: po + (half + 1) * 512],
                            start=False, stop=True),
                            waits=list(wb2r[eidx]) if half == 0 else (), inc=s_pe)
                    wmm = P.now(s_pe)
                    yr = mp % 2
                    for half in range(2):
                        pb = PB[bA + half]
                        P.op("scalar", lambda e, pb=pb, yr=yr, half=half: e.activation(
                            ys_sb[:, yr * D + half * 512: yr * D + (half + 1) * 512],
                            pb[:, :], AF.Copy, bias=0.0, scale=1.0),
                            waits=(wmm + P.now(s_w)) if half == 0 else (), inc=s_act)
                        bmark(bA + half, s_act)
                    wsc0 = wslz if first_scatter else []
                    first_scatter = False
                    P.op("gpsimd", lambda e, yr=yr, blk_i=blk_i, gofs=gofs:
                         e.indirect_dma_start(
                             slot_d,
                             IndirectOffsetOnAxis(
                                 ap=gsc_sb[:, gofs + blk_i: gofs + blk_i + 1],
                                 axis=0),
                             ys_sb[:, yr * D:(yr + 1) * D], None),
                         waits=P.now(s_act) + list(wsc0), inc=s_w, inc_n=16)
                h_mark[0] = P.now(s_pe)
                wxc = wxc_next
                if cc == NCC - 1 and eidx + 1 < NE:
                    wfr_w2[eidx + 1] = emit_wload(eidx + 1, "w2", P.now(s_pe))

        phf = P.now(s_pe, s_act, s_dve, s_w, s_x, s_wf, s_st)
        with contextlib.ExitStack() as _cs:
            cent = _cs.enter_context
            sl_sb = cent(nc.sbuf_tensor("sl_sb", [128, 6 * 2 * D], BF16))
            t0_sb = cent(nc.sbuf_tensor("t0_sb", [128, 6 * D], F32))
            tsum_sb = cent(nc.sbuf_tensor("tsum_sb", [128, 6 * D], BF16))
            fx_sb = cent(nc.sbuf_tensor("fx_sb", [128, 6 * D], F32))
            fo_sb = cent(nc.sbuf_tensor("fo_sb", [128, 6 * D], F32))
            fin_sb = cent(nc.sbuf_tensor("fin_sb", [1, 1], F32))
            RC = 6
            f_marks = [()] * RC
            fx_marks = [()] * RC
            for g in range(ng):
                s, rem = divmod(g, GSEQ)
                r = g % RC
                P.dma("sync", sl_sb[:, r * 2 * D:(r + 1) * 2 * D]
                      .rearrange("p (q d) -> p q d", q=2),
                      slot_d[2 * g * 128:2 * (g + 1) * 128, :]
                      .rearrange("(p q) d -> p q d", q=2),
                      s_x, waits=list(f_marks[r]) + (phf if g < RC else []))
                P.dma("scalar", fx_sb[:, r * D:(r + 1) * D]
                      .rearrange("p (kc n) -> p kc n", kc=KC),
                      xmid_d[s, :, rem * 128:(rem + 1) * 128]
                      .rearrange("(kc p) n -> p kc n", p=128),
                      s_g, waits=list(fx_marks[r]) + (phf if g < RC else []))
                wld = P.now(s_x, s_g)
                P.op("vector", lambda e, r=r, g=g: e.tensor_scalar_mul(
                    t0_sb[:, r * D:(r + 1) * D], sl_sb[:, r * 2 * D: r * 2 * D + D],
                    wv12_sb[:, g:g + 1]), waits=wld + (P.now(s_pe) if g < RC else []),
                    inc=s_dve)
                P.op("vector", lambda e, r=r, g=g: e.scalar_tensor_tensor(
                    tsum_sb[:, r * D:(r + 1) * D],
                    sl_sb[:, r * 2 * D + D: r * 2 * D + 2 * D],
                    wv12_sb[:, NG + g: NG + g + 1], t0_sb[:, r * D:(r + 1) * D],
                    OP.mult, OP.add), waits=P.now(s_dve), inc=s_dve)
                wts = P.now(s_dve)
                wb7 = bfree(7)
                for kcc in range(KC):
                    P.op("tensor", lambda e, r=r, kcc=kcc: e.transpose(
                        p7[:, kcc * 128:(kcc + 1) * 128],
                        tsum_sb[:, r * D + kcc * 128: r * D + (kcc + 1) * 128],
                        ident_bf[:, :]),
                        waits=(wts + wb7) if kcc == 0 else (), inc=s_pe)
                wtr = P.now(s_pe)
                P.op("vector", lambda e, r=r: e.tensor_add(
                    fo_sb[:, r * D:(r + 1) * D]
                    .rearrange("p (kc n) -> p kc n", kc=KC),
                    fx_sb[:, r * D:(r + 1) * D]
                    .rearrange("p (kc n) -> p kc n", kc=KC),
                    p7[:, :].rearrange("p (kc n) -> p kc n", kc=KC)),
                    waits=wtr, inc=s_dve)
                bmark(7, s_dve)
                fx_marks[r] = P.now(s_dve)
                P.dma("gpsimd", outT[s, :, rem * 128:(rem + 1) * 128]
                      .rearrange("(kc p) n -> p kc n", p=128),
                      fo_sb[:, r * D:(r + 1) * D]
                      .rearrange("p (kc n) -> p kc n", kc=KC),
                      s_out, waits=P.now(s_dve))
                f_marks[r] = P.now(s_out)
            if dev_taps:
                P.dma("gpsimd", taps["t_xmid"], xmid_d, s_out,
                      waits=[("s_st", P.cnt["s_st"])])
                P.dma("gpsimd", taps["t_g"],
                      g_d.rearrange("(p n) one -> p (n one)", p=128), s_out,
                      waits=phf)
            P.op("vector", lambda e: e.memset(fin_sb[:, :], 0.0),
                 waits=P.now(s_out), inc=s_dve)

        P.run(blk)
    P.finish()
    return nc


_NC_CACHE = {}


def _prep(inputs):
    wo = np.asarray(inputs["L0_attn_Wo"], np.float32)
    woPK = wo.T.reshape(NH, DH, D).transpose(1, 0, 2)
    woMC = np.ascontiguousarray(
        woPK.reshape(DH, NH, KC, 128).transpose(2, 0, 1, 3))
    common = dict(
        wqT=np.ascontiguousarray(np.asarray(inputs["L0_attn_Wq"], np.float32).T),
        wkT=np.ascontiguousarray(np.asarray(inputs["L0_attn_Wk"], np.float32).T),
        wvT=np.ascontiguousarray(np.asarray(inputs["L0_attn_Wv"], np.float32).T),
        woMC=woMC,
        bq=np.ascontiguousarray(np.asarray(inputs["L0_attn_bq"], np.float32)
                                .reshape(KC, 128).T),
        bk=np.ascontiguousarray(np.asarray(inputs["L0_attn_bk"], np.float32)
                                .reshape(KC, 128).T),
        bvC=np.ascontiguousarray(np.asarray(inputs["L0_attn_bv"], np.float32)
                                 .reshape(NH, DH).T),
        ltriBD=np.ascontiguousarray(np.kron(
            np.eye(NE, dtype=np.float32),
            np.triu(np.ones((16, 16), dtype=np.float32), k=1))),
        bo=np.ascontiguousarray(np.asarray(inputs["L0_attn_bo"], np.float32)
                                .reshape(KC, 128).T),
        wrT=np.ascontiguousarray(np.asarray(inputs["L0_ffn_Wr"], np.float32).T),
        w1T=np.ascontiguousarray(np.transpose(
            np.asarray(inputs["L0_ffn_W1"], np.float32), (0, 2, 1))),
        b1=np.ascontiguousarray(np.asarray(inputs["L0_ffn_b1"], np.float32)
                                .reshape(NE, 32, 128).transpose(2, 0, 1)),
        w2T=np.ascontiguousarray(np.transpose(
            np.asarray(inputs["L0_ffn_W2"], np.float32), (0, 2, 1))),
        b2=np.ascontiguousarray(np.asarray(inputs["L0_ffn_b2"], np.float32)
                                .reshape(NE, KC, 128).transpose(2, 0, 1)),
        b2r=np.ascontiguousarray(np.asarray(inputs["L0_ffn_b2"], np.float32)),
    )
    return common


def kernel(**inputs):
    if "full" not in _NC_CACHE:
        _NC_CACHE["full"] = build_core_program()
    nc = _NC_CACHE["full"]
    x = np.asarray(inputs["L0_attn_x"], dtype=np.float32)
    common = _prep(inputs)
    in_maps = []
    for cid in range(NCORES):
        xs = x[cid * NSEQ:(cid + 1) * NSEQ]
        m = dict(common)
        m["xT"] = np.ascontiguousarray(np.transpose(xs, (0, 2, 1)))
        in_maps.append(m)
    res = run_bass_kernel_spmd(nc, in_maps, core_ids=list(range(NCORES)))
    outs = []
    for cid in range(NCORES):
        oT = np.asarray(res.results[cid]["outT"])
        outs.append(np.transpose(oT, (0, 2, 1)))
    return np.ascontiguousarray(np.concatenate(outs, axis=0))


# version 12
# speedup vs baseline: 1.0038x; 1.0038x over previous
import contextlib
import numpy as np
import ml_dtypes
import concourse.bass as bass
import concourse.mybir as mybir
from concourse.bass_core_rust import IndirectOffsetOnAxis
from concourse.bass_utils import run_bass_kernel_spmd

F32 = mybir.dt.float32
BF16 = mybir.dt.bfloat16
I32 = mybir.dt.int32
AF = mybir.ActivationFunctionType
OP = mybir.AluOpType
AX = mybir.AxisListType

B, L, D = 32, 2048, 1024
NCORES = 8
NSEQ = B // NCORES
W = 512
NCH = L // W
NH, DH = 16, 64
DFF, NE = 4096, 8
QSCALE = 0.125
EPS = 1e-5
KC = D // 128
TOK = NSEQ * L
NG = TOK // 128
GSEQ = L // 128
VROW = NH * 65
CAPS = 1152
CAPE = 4 * CAPS
NBLK = CAPE // 128
GMAGIC = float(1 << 20)
FFN_MODE = "sparse"

DEV_TAPS = False


class Prog:

    ENGINES = ("sync", "scalar", "vector", "tensor", "gpsimd")

    def __init__(self, nc):
        self.nc = nc
        self.ops = {e: [] for e in self.ENGINES}
        self.cnt = {}
        self.sems = {}
        self._ctx = []

    def sem(self, name):
        ctx = self.nc.semaphore(name)
        self.sems[name] = ctx.__enter__()
        self._ctx.append(ctx)
        self.cnt[name] = 0
        return name

    def now(self, *names):
        return [(n, self.cnt[n]) for n in names]

    def op(self, eng, build, waits=(), inc=None, inc_n=1):
        waits = tuple((s, v) for (s, v) in waits if v > 0)
        sem = self.sems[inc] if inc is not None else None

        def fn(e, build=build, waits=waits, sem=sem, inc_n=inc_n):
            for (s, v) in waits:
                e.wait_ge(self.sems[s], v)
            ins = build(e)
            if sem is not None:
                ins.then_inc(sem, inc_n)

        if inc is not None:
            self.cnt[inc] += inc_n
        self.ops[eng].append(fn)

    def dma(self, eng, out, in_, sem, waits=(), **kw):
        self.op(eng, lambda e: e.dma_start(out, in_, **kw), waits=waits,
                inc=sem, inc_n=16)

    def run(self, block):
        for name in self.ENGINES:
            if self.ops[name]:
                body = self.ops[name]

                def make(body=body):
                    def _(eng):
                        for fn in body:
                            fn(eng)
                    return _

                getattr(block, name)(make())

    def finish(self):
        for ctx in reversed(self._ctx):
            ctx.__exit__(None, None, None)


def build_core_program(n_seq=NSEQ, dev_taps=DEV_TAPS):
    nc = bass.Bass("TRN2", target_bir_lowering=False)
    tok = n_seq * L
    ng = tok // 128

    xT = nc.dram_tensor("xT", [n_seq, D, L], F32, kind="ExternalInput").ap()
    wqT = nc.dram_tensor("wqT", [D, D], F32, kind="ExternalInput").ap()
    wkT = nc.dram_tensor("wkT", [D, D], F32, kind="ExternalInput").ap()
    wvT = nc.dram_tensor("wvT", [D, D], F32, kind="ExternalInput").ap()
    woMC = nc.dram_tensor("woMC", [KC, DH, NH, 128], F32, kind="ExternalInput").ap()
    bq = nc.dram_tensor("bq", [128, KC], F32, kind="ExternalInput").ap()
    bk = nc.dram_tensor("bk", [128, KC], F32, kind="ExternalInput").ap()
    bvC = nc.dram_tensor("bvC", [64, NH], F32, kind="ExternalInput").ap()
    bo = nc.dram_tensor("bo", [128, KC], F32, kind="ExternalInput").ap()
    ltriBD = nc.dram_tensor("ltriBD", [128, 128], F32, kind="ExternalInput").ap()
    wrT = nc.dram_tensor("wrT", [D, NE], F32, kind="ExternalInput").ap()
    w1T = nc.dram_tensor("w1T", [NE, D, DFF], F32, kind="ExternalInput").ap()
    b1 = nc.dram_tensor("b1", [128, NE, 32], F32, kind="ExternalInput").ap()
    w2T = nc.dram_tensor("w2T", [NE, DFF, D], F32, kind="ExternalInput").ap()
    b2 = nc.dram_tensor("b2", [128, NE, KC], F32, kind="ExternalInput").ap()
    b2r = nc.dram_tensor("b2r", [NE, D], F32, kind="ExternalInput").ap()
    outT = nc.dram_tensor("outT", [n_seq, D, L], F32, kind="ExternalOutput").ap()

    xmid_d = nc.dram_tensor("xmid_d", [n_seq, D, L], F32).ap()
    ln2tok_d = nc.dram_tensor("ln2tok_d", [tok, D], BF16).ap()
    wobf = nc.dram_tensor("wobf", [KC, DH, NH, 128], BF16).ap()
    w1bf = nc.dram_tensor("w1bf", [NE, D, DFF], BF16).ap()
    w2bf = nc.dram_tensor("w2bf", [NE, DFF, D], BF16).ap()
    g_d = nc.dram_tensor("g_d", [NE * n_seq * CAPS, 1], F32).ap()
    slot_d = nc.dram_tensor("slot_d", [2 * tok + 128, D], BF16).ap()

    taps = {}
    if dev_taps:
        taps["t_xmid"] = nc.dram_tensor("t_xmid", [n_seq, D, L], F32,
                                        kind="ExternalOutput").ap()
        taps["t_logits"] = nc.dram_tensor("t_logits", [128, ng * NE], F32,
                                          kind="ExternalOutput").ap()
        taps["t_wt"] = nc.dram_tensor("t_wt", [128, NE * NG], F32,
                                      kind="ExternalOutput").ap()
        taps["t_g"] = nc.dram_tensor("t_g", [128, NE * n_seq * CAPS // 128], F32,
                                     kind="ExternalOutput").ap()

    P = Prog(nc)
    s_x = P.sem("s_x")
    s_w = P.sem("s_w")
    s_gp = P.sem("s_gp")
    s_wc = P.sem("s_wc")
    s_wf = P.sem("s_wf")
    s_g = P.sem("s_g")
    s_pe = P.sem("s_pe")
    s_act = P.sem("s_act")
    s_dve = P.sem("s_dve")
    s_st = P.sem("s_st")
    s_out = P.sem("s_out")

    bank_marks = [()] * 8

    with contextlib.ExitStack() as _es:
        ent = _es.enter_context
        ident_f = ent(nc.sbuf_tensor("ident_f", [128, 128], F32))
        ident_bf = ent(nc.sbuf_tensor("ident_bf", [128, 128], BF16))
        ones_col_bf = ent(nc.sbuf_tensor("ones_col_bf", [128, 1], BF16))
        ones_row_bf = ent(nc.sbuf_tensor("ones_row_bf", [1, 128], BF16))
        b1_sb = ent(nc.sbuf_tensor("b1_sb", [128, NE * 32], F32))
        b2_sb = ent(nc.sbuf_tensor("b2_sb", [128, NE * KC], F32))
        st_sb = ent(nc.sbuf_tensor("st_sb", [1, 3 * W], F32))
        stbf_sb = ent(nc.sbuf_tensor("stbf_sb", [1, 2 * W + 4], BF16))
        sm_sb = ent(nc.sbuf_tensor("sm_sb", [128, 4], F32))
        eps_sb = ent(nc.sbuf_tensor("eps_sb", [128, 1], F32))
        wv12_sb = ent(nc.sbuf_tensor("wv12_sb", [128, 2 * NG], F32))
        PBh = [ent(nc.psum_tensor(f"p{i}", [128, 512], F32)) for i in range(7)]
        p7 = ent(nc.psum_tensor("p7", [128, 1024], BF16))
        p0, p1, p2, p3, p4, p5, p6 = PBh
        blk = ent(nc.Block())
        (lambda: None)()
        PB = PBh

        def bfree(i):
            return list(bank_marks[i])

        def bmark(i, sem):
            bank_marks[i] = P.now(sem)

        nc.const_aps.register(F32, EPS, eps_sb[:, 0:1])
        P.op("vector", lambda e: e.memset(eps_sb[:, :], EPS), inc=s_dve)
        P.op("vector", lambda e: e.memset(ident_f[:, :], 1.0), inc=s_dve)
        P.op("gpsimd", lambda e: e.affine_select(
            ident_f[:, :], ident_f[:, :], [[-1, 128]], OP.is_equal, 0.0,
            base=0, channel_multiplier=1), waits=P.now(s_dve), inc=s_gp, inc_n=1)
        P.op("vector", lambda e: e.memset(ident_bf[:, :], 1.0), inc=s_dve)
        P.op("gpsimd", lambda e: e.affine_select(
            ident_bf[:, :], ident_bf[:, :], [[-1, 128]], OP.is_equal, 0.0,
            base=0, channel_multiplier=1), waits=P.now(s_dve), inc=s_gp, inc_n=1)
        P.op("vector", lambda e: e.memset(ones_col_bf[:, :], 1.0), inc=s_dve)
        P.op("vector", lambda e: e.memset(ones_row_bf[:, :], 1.0), inc=s_dve)
        P.dma("gpsimd", wobf, woMC, s_wc)
        wocast_mark = P.now(s_wc)
        cast_marks = {}
        for eidx in range(NE):
            P.dma("gpsimd", w1bf[eidx], w1T[eidx], s_wc)
            P.dma("gpsimd", w2bf[eidx], w2T[eidx], s_wc)
            cast_marks[eidx] = P.now(s_wc)
        P.dma("sync", b1_sb[:, :].rearrange("p (e mc) -> p e mc", e=NE), b1, s_x)
        P.dma("sync", b2_sb[:, :].rearrange("p (e mc) -> p e mc", e=NE), b2, s_x)

        with contextlib.ExitStack() as _as:
            aent = _as.enter_context
            wq_sb = aent(nc.sbuf_tensor("wq_sb", [128, KC * D], BF16))
            wk_sb = aent(nc.sbuf_tensor("wk_sb", [128, KC * D], BF16))
            wv_sb = aent(nc.sbuf_tensor("wv_sb", [128, KC * D], BF16))
            wos_sb = aent(nc.sbuf_tensor("wos_sb", [64, 2 * NH * 128], BF16))
            wr_sb = aent(nc.sbuf_tensor("wr_sb", [128, KC * NE], BF16))
            bq_sb = aent(nc.sbuf_tensor("bq_sb", [128, KC], F32))
            bk_sb = aent(nc.sbuf_tensor("bk_sb", [128, KC], F32))
            bo_sb = aent(nc.sbuf_tensor("bo_sb", [128, KC], F32))
            bvc_sb = aent(nc.sbuf_tensor("bvc_sb", [64, NH], F32))
            x_sb = aent(nc.sbuf_tensor("x_sb", [128, 2 * KC * W], F32))
            xbf_sb = aent(nc.sbuf_tensor("xbf_sb", [128, KC * W], BF16))
            a_sb = aent(nc.sbuf_tensor("a_sb", [128, 2 * KC * W], BF16))
            xln_sb = aent(nc.sbuf_tensor("xln_sb", [128, KC * W], BF16))
            k_sb = aent(nc.sbuf_tensor("k_sb", [128, 2 * KC * W], BF16))
            v_sb = aent(nc.sbuf_tensor("v_sb", [128, 2 * 4 * VROW], BF16))
            o_sb = aent(nc.sbuf_tensor("o_sb", [64, NH * W], BF16))
            rbc_sb = aent(nc.sbuf_tensor("rbc_sb", [64, W], BF16))
            lt2_sb = aent(nc.sbuf_tensor("lt2_sb", [128, 2 * 1024], BF16))
            logits_sb = aent(nc.sbuf_tensor("logits_sb", [128, NG * NE], F32))
            rt_sb = aent(nc.sbuf_tensor("rt_sb", [128, 5 * 128 + 64], F32))
            ltbd_sb = aent(nc.sbuf_tensor("ltbd_sb", [128, 128], BF16))
            ptri_sb = aent(nc.sbuf_tensor("ptri_sb", [128, 128], F32))
            evals_sb = aent(nc.sbuf_tensor("evals_sb", [128, NE], F32))
            ids_sb = aent(nc.sbuf_tensor("ids_sb", [128, NG], F32))
            iti_sb = aent(nc.sbuf_tensor("iti_sb", [128, NG], I32))
            posall_sb = aent(nc.sbuf_tensor("posall_sb", [128, 16 * NE], F32))
            fe_all = aent(nc.sbuf_tensor("fe_all", [128, 128], BF16))
            ft_all = aent(nc.sbuf_tensor("ft_all", [128, 128], BF16))
            ct_all = aent(nc.sbuf_tensor("ct_all", [128, 128], F32))
            pm_all = aent(nc.sbuf_tensor("pm_all", [128, 128], F32))
            tot_sb = aent(nc.sbuf_tensor("tot_sb", [128, NE], F32))
            bcol8_sb = aent(nc.sbuf_tensor("bcol8_sb", [128, NE], F32))
            offi_sb = aent(nc.sbuf_tensor("offi_sb", [128, 32], I32))
            valb_sb = aent(nc.sbuf_tensor("valb_sb", [128, 16], F32))
            q_sb = xbf_sb
            ln2_sb = xln_sb

            for wsb, wdr in ((wq_sb, wqT), (wk_sb, wkT), (wv_sb, wvT)):
                P.dma("gpsimd", wsb[:, :].rearrange("p (kc m) -> p kc m", kc=KC),
                      wdr.rearrange("(kc p) m -> p kc m", p=128), s_w)
            P.dma("gpsimd", wr_sb[:, :].rearrange("p (kc e) -> p kc e", kc=KC),
                  wrT.rearrange("(kc p) e -> p kc e", p=128), s_w)
            for bsb, bdr in ((bq_sb, bq), (bk_sb, bk), (bo_sb, bo), (bvc_sb, bvC)):
                P.dma("sync", bsb[:, :], bdr, s_x)
            P.op("vector", lambda e: e.tensor_scalar_mul(
                bq_sb[:, :], bq_sb[:, :], QSCALE), waits=P.now(s_x), inc=s_dve)
            for r in range(2):
                ap = v_sb[:, r * 4 * VROW:(r + 1) * 4 * VROW].rearrange(
                    "p (x d) -> p x d", d=65)[:, :, 64:65]
                P.op("vector", lambda e, ap=ap: e.memset(ap, 1.0), inc=s_dve)
            P.dma("gpsimd", ltbd_sb[:, :], ltriBD, s_w)
            P.op("vector", lambda e: e.memset(ptri_sb[:, :], 1.0), inc=s_dve)
            P.op("gpsimd", lambda e: e.affine_select(
                ptri_sb[:, :], ptri_sb[:, :], [[1, 128]], OP.is_ge, 0.0,
                base=-1, channel_multiplier=-1), waits=P.now(s_dve), inc=s_gp, inc_n=1)
            P.op("gpsimd", lambda e: e.iota(iti_sb[:, 0:NE], [[1, NE]], base=0,
                                            channel_multiplier=0), inc=s_gp, inc_n=1)
            P.op("vector", lambda e: e.tensor_copy(evals_sb[:, :], iti_sb[:, 0:NE]),
                 waits=P.now(s_gp), inc=s_dve)
            P.op("gpsimd", lambda e: e.iota(iti_sb[:, :], [[128, NG]], base=0,
                                            channel_multiplier=1),
                 waits=P.now(s_dve), inc=s_gp, inc_n=1)
            P.op("vector", lambda e: e.tensor_copy(ids_sb[:, :], iti_sb[:, :]),
                 waits=P.now(s_gp), inc=s_dve)
            ginit_cols = NE * n_seq * CAPS // 128
            P.op("vector", lambda e: e.memset(logits_sb[:, 0:ginit_cols], GMAGIC),
                 inc=s_dve)
            P.dma("sync", g_d.rearrange("(p n) one -> p (n one)", p=128),
                  logits_sb[:, 0:ginit_cols], s_st, waits=P.now(s_dve))
            wginit = P.now(s_st)
            attw = P.now(s_w, s_gp, s_x, s_dve, s_pe)

            def xsl(ring, kc, a=0, b=W):
                off = ring * KC * W + kc * W
                return x_sb[:, off + a: off + b]

            xring_free = [[], []]

            def emit_xload(s, c):
                ring = (s * NCH + c) % 2
                w = list(xring_free[ring])
                P.dma("sync", x_sb[:, ring * KC * W:(ring + 1) * KC * W]
                      .rearrange("p (kc n) -> p kc n", kc=KC),
                      xT[s, :, c * W:(c + 1) * W].rearrange("(kc p) n -> p kc n", p=128),
                      s_x, waits=w)
                return P.now(s_x)

            loads = {(0, 0): emit_xload(0, 0)}

            def emit_ln(src_slices, out_sb, wsrc, wrows=None):
                if wrows is None:
                    wrows = P.now(s_act, s_pe)
                for kc in range(KC):
                    P.op("vector", lambda e, kc=kc: e.tensor_copy(
                        xbf_sb[:, kc * W:(kc + 1) * W], src_slices(kc)),
                        waits=wsrc if kc == 0 else (), inc=s_dve)
                wcp = P.now(s_dve)
                for kc in range(KC):
                    P.op("tensor", lambda e, kc=kc: e.matmul(
                        p5[0:1, :], ones_col_bf[:, 0:1],
                        xbf_sb[:, kc * W:(kc + 1) * W],
                        start=(kc == 0), stop=(kc == KC - 1)),
                        waits=(wcp + bfree(5)) if kc == 0 else (), inc=s_pe)
                wsum = P.now(s_pe)
                for kc in range(KC):
                    P.op("vector", lambda e, kc=kc: e.tensor_mul(
                        xbf_sb[:, kc * W:(kc + 1) * W],
                        xbf_sb[:, kc * W:(kc + 1) * W],
                        xbf_sb[:, kc * W:(kc + 1) * W]),
                        waits=wsum if kc == 0 else (), inc=s_dve)
                wsq = P.now(s_dve)
                for kc in range(KC):
                    P.op("tensor", lambda e, kc=kc: e.matmul(
                        p6[0:1, :], ones_col_bf[:, 0:1],
                        xbf_sb[:, kc * W:(kc + 1) * W],
                        start=(kc == 0), stop=(kc == KC - 1)),
                        waits=(wsq + bfree(6)) if kc == 0 else (), inc=s_pe)
                wst = P.now(s_pe)
                mrow = st_sb[0:1, 0:W]
                vrow = st_sb[0:1, W:2 * W]
                rrow = st_sb[0:1, 2 * W:3 * W]
                P.op("vector", lambda e: e.tensor_scalar_mul(mrow, p5[0:1, :], 1.0 / D),
                     waits=wst + wrows, inc=s_dve)
                P.op("vector", lambda e: e.tensor_scalar_mul(vrow, p6[0:1, :], 1.0 / D),
                     inc=s_dve)
                bmark(5, s_dve)
                bmark(6, s_dve)
                P.op("vector", lambda e: e.tensor_mul(rrow, mrow, mrow),
                     waits=P.now(s_dve), inc=s_dve)
                P.op("vector", lambda e: e.tensor_sub(vrow, vrow, rrow),
                     waits=P.now(s_dve), inc=s_dve)
                P.op("scalar", lambda e: e.activation(rrow, vrow, AF.Ln,
                                                      bias=EPS, scale=1.0),
                     waits=P.now(s_dve), inc=s_act)
                P.op("scalar", lambda e: e.activation(rrow, rrow, AF.Exp,
                                                      bias=0.0, scale=-0.5),
                     waits=P.now(s_act), inc=s_act)
                mbf = stbf_sb[0:1, 0:W]
                rbf = stbf_sb[0:1, W:2 * W]
                wr2 = P.now(s_act)
                P.op("vector", lambda e: e.tensor_copy(mbf, mrow), waits=wr2, inc=s_dve)
                P.op("vector", lambda e: e.tensor_copy(rbf, rrow), inc=s_dve)
                wb_ = P.now(s_dve)
                P.op("tensor", lambda e: e.matmul(p5[:, :], ones_row_bf[0:1, :], mbf,
                                                  start=True, stop=True),
                     waits=wb_ + bfree(5), inc=s_pe)
                P.op("tensor", lambda e: e.matmul(p6[:, :], ones_row_bf[0:1, :], rbf,
                                                  start=True, stop=True),
                     waits=bfree(6), inc=s_pe)
                wbc = P.now(s_pe)
                for kc in range(KC):
                    P.op("vector", lambda e, kc=kc: e.tensor_sub(
                        xbf_sb[:, kc * W:(kc + 1) * W], src_slices(kc), p5[:, :]),
                        waits=wbc if kc == 0 else (), inc=s_dve)
                    P.op("vector", lambda e, kc=kc: e.tensor_mul(
                        out_sb[:, kc * W:(kc + 1) * W],
                        xbf_sb[:, kc * W:(kc + 1) * W], p6[:, :]),
                        waits=P.now(s_dve), inc=s_dve)
                bmark(5, s_dve)
                bmark(6, s_dve)
                return P.now(s_dve)

            def emit_proj_fm(wsb, bias_sb, out_sb, wln, scale):
                for mc in range(KC):
                    pb = PB[mc % 2]
                    wb = bfree(mc % 2)
                    for kcc in range(KC):
                        P.op("tensor", lambda e, mc=mc, kcc=kcc, pb=pb: e.matmul(
                            pb[:, :],
                            wsb[:, kcc * D + mc * 128: kcc * D + (mc + 1) * 128],
                            xln_sb[:, kcc * W:(kcc + 1) * W],
                            start=(kcc == 0), stop=(kcc == KC - 1)),
                            waits=(wln + wb + attw) if kcc == 0 else (), inc=s_pe)
                    wmm = P.now(s_pe)
                    P.op("scalar", lambda e, mc=mc, pb=pb: e.activation(
                        out_sb[:, mc * W:(mc + 1) * W], pb[:, :], AF.Identity,
                        bias=bias_sb[:, mc:mc + 1], scale=scale),
                        waits=wmm, inc=s_act)
                    bmark(mc % 2, s_act)
                return P.now(s_act)

            def emit_proj_v(vr, wln):
                base = vr * 4 * VROW
                for tc in range(4):
                    for half in range(2):
                        bi = (tc * 2 + half) % 2
                        pb = PB[bi]
                        wb = bfree(bi)
                        for kcc in range(KC):
                            P.op("tensor", lambda e, tc=tc, half=half, kcc=kcc, pb=pb: e.matmul(
                                pb[:, :],
                                xln_sb[:, kcc * W + tc * 128: kcc * W + (tc + 1) * 128],
                                wv_sb[:, kcc * D + half * 512: kcc * D + (half + 1) * 512],
                                start=(kcc == 0), stop=(kcc == KC - 1)),
                                waits=(wln + wb + attw) if kcc == 0 else (), inc=s_pe)
                        wmm = P.now(s_pe)
                        off = base + tc * VROW + half * 8 * 65
                        dst = v_sb[:, off: off + 8 * 65].rearrange(
                            "p (s d) -> p s d", d=65)[:, :, 0:64]
                        P.op("vector", lambda e, pb=pb, dst=dst: e.tensor_copy(
                            dst, pb[:, :].rearrange("p (s d) -> p s d", d=64)),
                            waits=wmm, inc=s_dve)
                        bmark(bi, s_dve)
                return P.now(s_dve)

            def emit_attention(c, negc_ready):
                rings = [c % 2] if c == 0 else [(c - 1) % 2, c % 2]
                nsub = 4 * len(rings)
                wqkv = P.now(s_act, s_dve)
                exp_done = {}
                av_done = {}

                def escore(h):
                    nonlocal negc_ready
                    hb = (h % 2) * 64
                    hc = h // 2
                    aoff = (h % 2) * KC * W
                    for j in range(nsub):
                        ring = rings[j // 4]
                        tj = j % 4
                        bi = 1 + (h * nsub + j) % 3
                        pb = PB[bi]
                        wb = bfree(bi)
                        P.op("tensor", lambda e, ring=ring, tj=tj, pb=pb, hb=hb, hc=hc: e.matmul(
                            pb[:, :],
                            k_sb[hb:hb + 64, ring * KC * W + hc * W + tj * 128:
                                 ring * KC * W + hc * W + (tj + 1) * 128],
                            q_sb[hb:hb + 64, hc * W:(hc + 1) * W],
                            start=True, stop=True),
                            waits=wqkv + wb, inc=s_pe)
                        wmm = P.now(s_pe)
                        if negc_ready is None:
                            P.op("vector", lambda e, pb=pb: e.reduce_max(
                                sm_sb[:, 0:1], pb[:, :], axis=AX.X),
                                waits=wmm, inc=s_dve)
                            P.op("tensor", lambda e: e.transpose(
                                p5[0:1, 0:128], sm_sb[:, 0:1], ident_f[:, :]),
                                waits=P.now(s_dve) + bfree(5), inc=s_pe)
                            P.op("vector", lambda e: e.reduce_max(
                                sm_sb[0:1, 1:2], p5[0:1, 0:128], axis=AX.X),
                                waits=P.now(s_pe), inc=s_dve)
                            bmark(5, s_dve)
                            P.op("vector", lambda e: e.tensor_scalar_mul(
                                sm_sb[0:1, 1:2], sm_sb[0:1, 1:2], -1.0),
                                waits=P.now(s_dve), inc=s_dve)
                            P.op("vector", lambda e: e.tensor_copy(
                                stbf_sb[0:1, 2 * W:2 * W + 1], sm_sb[0:1, 1:2]),
                                waits=P.now(s_dve), inc=s_dve)
                            P.op("tensor", lambda e: e.matmul(
                                p5[:, 0:1], ones_row_bf[0:1, :],
                                stbf_sb[0:1, 2 * W:2 * W + 1],
                                start=True, stop=True),
                                waits=P.now(s_dve) + bfree(5), inc=s_pe)
                            P.op("vector", lambda e: e.tensor_copy(
                                sm_sb[:, 2:3], p5[:, 0:1]),
                                waits=P.now(s_pe), inc=s_dve)
                            bmark(5, s_dve)
                            negc_ready = P.now(s_dve)
                        P.op("scalar", lambda e, j=j, pb=pb, aoff=aoff: e.activation(
                            a_sb[:, aoff + j * W: aoff + (j + 1) * W], pb[:, :],
                            AF.Exp, bias=sm_sb[:, 2:3], scale=1.0),
                            waits=P.now(s_pe) + negc_ready, inc=s_act)
                        bmark(bi, s_act)
                    exp_done[h] = P.now(s_act)

                def eav(h):
                    bi = 4 + h % 3
                    pbav = PB[bi]
                    wavb = bfree(bi)
                    aoff = (h % 2) * KC * W
                    for j in range(nsub):
                        ring = rings[j // 4]
                        tj = j % 4
                        voff = ring * 4 * VROW + tj * VROW + h * 65
                        P.op("tensor", lambda e, j=j, voff=voff, pbav=pbav, aoff=aoff:
                             e.matmul(
                                 pbav[0:65, :], v_sb[:, voff: voff + 65],
                                 a_sb[:, aoff + j * W: aoff + (j + 1) * W],
                                 start=(j == 0), stop=(j == nsub - 1)),
                             waits=(exp_done[h] + wavb + attw) if j == 0 else (),
                             inc=s_pe)
                    av_done[h] = P.now(s_pe)

                def etail(h):
                    bi = 4 + h % 3
                    pbav = PB[bi]
                    par = h % 2
                    lr = st_sb[0:1, par * W:(par + 1) * W]
                    er = stbf_sb[0:1, par * W:(par + 1) * W]
                    P.op("scalar", lambda e, pbav=pbav, lr=lr: e.activation(
                        lr, pbav[64:65, :], AF.Ln, bias=0.0, scale=1.0),
                        waits=av_done[h], inc=s_act)
                    P.op("scalar", lambda e, lr=lr, er=er: e.activation(
                        er, lr, AF.Exp, bias=0.0, scale=-1.0),
                        waits=P.now(s_act), inc=s_act)
                    P.op("tensor", lambda e, er=er: e.matmul(
                        p0[0:64, :], ones_row_bf[0:1, 0:64], er,
                        start=True, stop=True),
                        waits=P.now(s_act) + bfree(0), inc=s_pe)
                    P.op("vector", lambda e: e.tensor_copy(rbc_sb[:, :], p0[0:64, :]),
                         waits=P.now(s_pe), inc=s_dve)
                    bmark(0, s_dve)
                    P.op("vector", lambda e, h=h, pbav=pbav: e.tensor_mul(
                        o_sb[0:64, h * W:(h + 1) * W], pbav[0:64, :], rbc_sb[:, :]),
                        waits=P.now(s_dve), inc=s_dve)
                    P.op("vector", lambda e, h=h: e.tensor_scalar_add(
                        o_sb[0:64, h * W:(h + 1) * W],
                        o_sb[0:64, h * W:(h + 1) * W], bvc_sb[0:64, h:h + 1]),
                        waits=P.now(s_dve), inc=s_dve)
                    bmark(bi, s_dve)

                escore(0)
                escore(1)
                eav(0)
                for h in range(2, NH):
                    escore(h)
                    eav(h - 1)
                    etail(h - 2)
                qfree = P.now(s_pe)
                eav(NH - 1)
                etail(NH - 2)
                etail(NH - 1)
                return P.now(s_dve), negc_ready, qfree

            wos_marks = [(), ()]

            def emit_outproj_residual(ring, watt):
                wslab = {}
                for mc in range(2):
                    P.dma("sync", wos_sb[:, (mc % 2) * NH * 128:(mc % 2 + 1) * NH * 128]
                          .rearrange("d (h n) -> d h n", h=NH), wobf[mc],
                          s_x, waits=list(wos_marks[mc % 2]) + list(wocast_mark))
                    wslab[mc] = P.now(s_x)
                for mc in range(KC):
                    bi = mc % 2
                    pb = PB[bi]
                    wb = bfree(bi)
                    for h in range(NH):
                        P.op("tensor", lambda e, mc=mc, h=h, pb=pb: e.matmul(
                            pb[:, :],
                            wos_sb[0:64, (mc % 2) * NH * 128 + h * 128:
                                   (mc % 2) * NH * 128 + (h + 1) * 128],
                            o_sb[0:64, h * W:(h + 1) * W],
                            start=(h == 0), stop=(h == NH - 1)),
                            waits=(watt + wb + wslab[mc]) if h == 0 else (), inc=s_pe)
                    wos_marks[mc % 2] = P.now(s_pe)
                    if mc + 2 < KC:
                        P.dma("sync", wos_sb[:, (mc % 2) * NH * 128:(mc % 2 + 1) * NH * 128]
                              .rearrange("d (h n) -> d h n", h=NH), wobf[mc + 2],
                              s_x, waits=list(wos_marks[mc % 2]))
                        wslab[mc + 2] = P.now(s_x)
                    wmm = P.now(s_pe)
                    P.op("vector", lambda e, mc=mc, pb=pb, ring=ring: e.scalar_tensor_tensor(
                        xsl(ring, mc), pb[:, :], bo_sb[:, mc:mc + 1],
                        xsl(ring, mc), OP.add, OP.add), waits=wmm, inc=s_dve)
                    bmark(bi, s_dve)
                return P.now(s_dve)

            def emit_router_logits(s, c, wln2):
                wb = bfree(5)
                for tc in range(4):
                    for kcc in range(KC):
                        P.op("tensor", lambda e, tc=tc, kcc=kcc: e.matmul(
                            p5[:, tc * NE:(tc + 1) * NE],
                            ln2_sb[:, kcc * W + tc * 128: kcc * W + (tc + 1) * 128],
                            wr_sb[:, kcc * NE:(kcc + 1) * NE],
                            start=(kcc == 0), stop=(kcc == KC - 1)),
                            waits=(wln2 + wb + attw) if (tc == 0 and kcc == 0) else (),
                            inc=s_pe)
                wmm = P.now(s_pe)
                g0 = s * GSEQ + c * 4
                wg0 = wginit if (s == 0 and c == 0) else []
                P.op("vector", lambda e, g0=g0: e.tensor_copy(
                    logits_sb[:, g0 * NE:(g0 + 4) * NE], p5[:, 0:4 * NE]),
                    waits=wmm + list(wg0), inc=s_dve)
                bmark(5, s_dve)

            lt2_marks = [(), ()]

            def emit_ln2_tok_store(s, c, wln2):
                for tc in range(4):
                    lr = tc % 2
                    wb = bfree(7)
                    for kcc in range(KC):
                        P.op("tensor", lambda e, tc=tc, kcc=kcc: e.transpose(
                            p7[:, kcc * 128:(kcc + 1) * 128],
                            ln2_sb[:, kcc * W + tc * 128: kcc * W + (tc + 1) * 128],
                            ident_bf[:, :]),
                            waits=(wln2 + wb) if kcc == 0 else (), inc=s_pe)
                    wtr = P.now(s_pe)
                    P.op("vector", lambda e, lr=lr: e.tensor_copy(
                        lt2_sb[:, lr * 1024:(lr + 1) * 1024], p7[:, :]),
                        waits=wtr + list(lt2_marks[lr]), inc=s_dve)
                    bmark(7, s_dve)
                    r0 = s * L + c * W + tc * 128
                    P.dma("sync", ln2tok_d[r0:r0 + 128, :],
                          lt2_sb[:, lr * 1024:(lr + 1) * 1024], s_st,
                          waits=P.now(s_dve))
                    lt2_marks[lr] = P.now(s_st)

            RT_D, RT_EQ, RT_MK, RT_SEL, RT_S = 0, 128, 256, 384, 512
            RT_M1, RT_M2D, RT_W1, RT_W2 = 640, 656, 672, 688

            def rt(a, n=128):
                return rt_sb[:, a:a + n]

            def rt3(a, n=128):
                return rt_sb[:, a:a + n].rearrange("p (g e) -> p g e", e=NE)

            def rtb(a):
                return rt_sb[:, a:a + 16].rearrange("p (g one) -> p g one", one=1) \
                    .broadcast_to([128, 16, NE])

            def emit_routing_masks(s):
                g0 = s * GSEQ
                l3 = logits_sb[:, g0 * NE:(g0 + GSEQ) * NE].rearrange(
                    "p (g e) -> p g e", e=NE)
                P.op("vector", lambda e, l3=l3: e.tensor_reduce(
                    rt(RT_M1, 16).rearrange("p (g one) -> p g one", one=1),
                    l3, AX.X, OP.max), waits=P.now(s_pe), inc=s_dve)
                P.op("vector", lambda e, l3=l3: e.tensor_sub(
                    rt3(RT_D), l3, rtb(RT_M1)), waits=P.now(s_dve), inc=s_dve)
                P.op("vector", lambda e: e.tensor_single_scalar(
                    rt(RT_EQ), rt(RT_D), 0.0, OP.is_ge),
                    waits=P.now(s_dve), inc=s_dve)
                P.op("vector", lambda e: e.scalar_tensor_tensor(
                    rt(RT_MK), rt(RT_EQ), -1e9, rt(RT_D), OP.mult, OP.add),
                    waits=P.now(s_dve), inc=s_dve)
                P.op("vector", lambda e: e.tensor_reduce(
                    rt(RT_M2D, 16).rearrange("p (g one) -> p g one", one=1),
                    rt3(RT_MK), AX.X, OP.max), waits=P.now(s_dve), inc=s_dve)
                P.op("scalar", lambda e: e.activation(
                    rt(RT_W1, 16), rt(RT_M2D, 16), AF.Exp, bias=0.0, scale=1.0),
                    waits=P.now(s_dve), inc=s_act)
                P.op("vector", lambda e: e.tensor_scalar_add(
                    rt(RT_W1, 16), rt(RT_W1, 16), 1.0), waits=P.now(s_act), inc=s_dve)
                P.op("vector", lambda e: e.reciprocal(
                    rt(RT_W1, 16), rt(RT_W1, 16)), waits=P.now(s_dve), inc=s_dve)
                P.op("vector", lambda e: e.tensor_scalar(
                    rt(RT_W2, 16), rt(RT_W1, 16), -1.0, 1.0, OP.mult, OP.add),
                    waits=P.now(s_dve), inc=s_dve)
                P.op("vector", lambda e: e.tensor_tensor(
                    rt3(RT_SEL), rt3(RT_D), rtb(RT_M2D), OP.is_ge),
                    waits=P.now(s_dve), inc=s_dve)
                P.op("vector", lambda e: e.tensor_sub(
                    rt(RT_S), rt(RT_SEL), rt(RT_EQ)), waits=P.now(s_dve), inc=s_dve)
                P.op("vector", lambda e, s=s: e.tensor_copy(
                    wv12_sb[:, s * 16:(s + 1) * 16], rt(RT_W1, 16)),
                    waits=P.now(s_dve), inc=s_dve)
                P.op("vector", lambda e, s=s: e.tensor_copy(
                    wv12_sb[:, NG + s * 16: NG + (s + 1) * 16], rt(RT_W2, 16)),
                    inc=s_dve)
                P.op("vector", lambda e: e.tensor_copy(
                    fe_all[:, :].rearrange("p (e g) -> p e g", g=GSEQ),
                    rt3(RT_SEL).transpose([0, 2, 1])),
                    waits=P.now(s_dve) + P.now(s_pe), inc=s_dve)

            def emit_routing_pos(s):
                P.op("tensor", lambda e: e.transpose(
                    p7[:, 0:128], fe_all[:, :], ident_bf[:, :]),
                    waits=P.now(s_dve) + bfree(7), inc=s_pe)
                P.op("vector", lambda e: e.tensor_copy(
                    ft_all[:, :], p7[:, 0:128]), waits=P.now(s_pe), inc=s_dve)
                bmark(7, s_dve)
                P.op("tensor", lambda e: e.matmul(
                    p5[:, 0:128], ltbd_sb[:, :], ft_all[:, :],
                    start=True, stop=True),
                    waits=P.now(s_dve) + bfree(5), inc=s_pe)
                P.op("vector", lambda e: e.tensor_copy(
                    ct_all[:, :], p5[:, 0:128]), waits=P.now(s_pe), inc=s_dve)
                bmark(5, s_dve)
                P.op("tensor", lambda e: e.transpose(
                    p6[:, 0:128], ct_all[:, :], ident_f[:, :]),
                    waits=P.now(s_dve) + bfree(6), inc=s_pe)
                P.op("vector", lambda e: e.tensor_copy(
                    pm_all[:, :], p6[:, 0:128]), waits=P.now(s_pe), inc=s_dve)
                bmark(6, s_dve)
                P.op("vector", lambda e: e.tensor_reduce(
                    tot_sb[:, :].rearrange("p (e one) -> p e one", one=1),
                    rt3(RT_SEL).transpose([0, 2, 1]), AX.X, OP.add),
                    waits=P.now(s_dve), inc=s_dve)
                P.op("tensor", lambda e: e.matmul(
                    p5[:, 0:NE], ptri_sb[:, :], tot_sb[:, :],
                    start=True, stop=True),
                    waits=P.now(s_dve) + bfree(5), inc=s_pe)
                P.op("vector", lambda e: e.tensor_copy(
                    bcol8_sb[:, :], p5[:, 0:NE]), waits=P.now(s_pe), inc=s_dve)
                bmark(5, s_dve)
                P.op("vector", lambda e: e.tensor_add(
                    posall_sb[:, :].rearrange("p (g e) -> p g e", e=NE),
                    pm_all[:, :].rearrange("p (e g) -> p e g", g=GSEQ)
                    .transpose([0, 2, 1]),
                    bcol8_sb[:, :].rearrange("p (one e) -> p one e", one=1)
                    .broadcast_to([128, GSEQ, NE])),
                    waits=P.now(s_dve), inc=s_dve)
                pos3 = posall_sb[:, :].rearrange("p (g e) -> p g e", e=NE)
                eb = evals_sb[:, :].rearrange("p (one e) -> p one e", one=1) \
                    .broadcast_to([128, 16, NE])
                SC = RT_D
                for i, (msk, dst) in enumerate(
                        ((RT_EQ, SC), (RT_EQ, SC + 16), (RT_S, SC + 32),
                         (RT_S, SC + 48))):
                    src_b = eb if i % 2 == 0 else pos3
                    P.op("vector", lambda e, msk=msk, src_b=src_b: e.tensor_mul(
                        rt3(RT_MK), rt3(msk), src_b),
                        waits=P.now(s_dve), inc=s_dve)
                    P.op("vector", lambda e, dst=dst: e.tensor_reduce(
                        rt(dst, 16).rearrange("p (g one) -> p g one", one=1),
                        rt3(RT_MK), AX.X, OP.add), waits=P.now(s_dve), inc=s_dve)
                for dst in (SC + 16, SC + 48):
                    P.op("vector", lambda e, dst=dst: e.tensor_scalar_min(
                        rt(dst, 16), rt(dst, 16), float(CAPS - 1)),
                        waits=P.now(s_dve), inc=s_dve)
                for ei, pi, oi in ((SC, SC + 16, 0), (SC + 32, SC + 48, 16)):
                    P.op("vector", lambda e, ei=ei, s=s: e.tensor_scalar(
                        rt(ei, 16), rt(ei, 16), float(n_seq), float(s),
                        OP.mult, OP.add), waits=P.now(s_dve), inc=s_dve)
                    P.op("vector", lambda e, ei=ei: e.tensor_scalar_mul(
                        rt(ei, 16), rt(ei, 16), float(CAPS)),
                        waits=P.now(s_dve), inc=s_dve)
                    P.op("vector", lambda e, ei=ei, pi=pi: e.tensor_add(
                        rt(ei, 16), rt(ei, 16), rt(pi, 16)),
                        waits=P.now(s_dve), inc=s_dve)
                    P.op("vector", lambda e, ei=ei, oi=oi: e.tensor_copy(
                        offi_sb[:, oi:oi + 16], rt(ei, 16)),
                        waits=P.now(s_dve) + P.now(s_w), inc=s_dve)
                P.op("vector", lambda e, s=s: e.tensor_scalar_add(
                    valb_sb[:, :], ids_sb[:, s * 16:(s + 1) * 16], 8192.0),
                    waits=P.now(s_dve) + P.now(s_w), inc=s_dve)
                woff = P.now(s_dve)
                for g in range(16):
                    P.op("gpsimd", lambda e, g=g, s=s: e.indirect_dma_start(
                        g_d, IndirectOffsetOnAxis(ap=offi_sb[:, g:g + 1], axis=0),
                        ids_sb[:, s * 16 + g: s * 16 + g + 1], None),
                        waits=(woff + P.now(s_st)) if g == 0 else (),
                        inc=s_w, inc_n=16)
                    P.op("gpsimd", lambda e, g=g: e.indirect_dma_start(
                        g_d, IndirectOffsetOnAxis(ap=offi_sb[:, 16 + g:17 + g], axis=0),
                        valb_sb[:, g:g + 1], None),
                        inc=s_w, inc_n=16)

            wln = None
            for s in range(n_seq):
                negc_ready = None
                for c in range(NCH):
                    ring = (s * NCH + c) % 2
                    nxt = (s, c + 1) if c + 1 < NCH else (
                        (s + 1, 0) if s + 1 < n_seq else None)
                    if wln is None:
                        loads_cur = loads.pop((s, c))
                        wln = emit_ln(lambda kc, ring=ring: xsl(ring, kc), xln_sb,
                                      loads_cur + P.now(s_pe, s_act, s_dve, s_st))
                    kr = c % 2
                    for mc in range(KC):
                        pb = PB[mc % 2]
                        wb = bfree(mc % 2)
                        for kcc in range(KC):
                            P.op("tensor", lambda e, mc=mc, kcc=kcc, pb=pb: e.matmul(
                                pb[:, :],
                                wk_sb[:, kcc * D + mc * 128: kcc * D + (mc + 1) * 128],
                                xln_sb[:, kcc * W:(kcc + 1) * W],
                                start=(kcc == 0), stop=(kcc == KC - 1)),
                                waits=(wln + wb + attw + P.now(s_pe)) if kcc == 0 else (),
                                inc=s_pe)
                        wmm = P.now(s_pe)
                        P.op("scalar", lambda e, mc=mc, pb=pb, kr=kr: e.activation(
                            k_sb[:, kr * KC * W + mc * W: kr * KC * W + (mc + 1) * W],
                            pb[:, :], AF.Identity, bias=bk_sb[:, mc:mc + 1], scale=1.0),
                            waits=wmm, inc=s_act)
                        bmark(mc % 2, s_act)
                    emit_proj_v(kr, wln)
                    emit_proj_fm(wq_sb, bq_sb, q_sb, wln + P.now(s_dve), QSCALE)
                    watt, negc_ready, qfree = emit_attention(c, negc_ready)
                    if c == 0 and s > 0:
                        emit_routing_pos(s - 1)
                    wmid = emit_outproj_residual(ring, watt)
                    P.dma("sync", xmid_d[s, :, c * W:(c + 1) * W]
                          .rearrange("(kc p) n -> p kc n", p=128),
                          x_sb[:, ring * KC * W:(ring + 1) * KC * W]
                          .rearrange("p (kc n) -> p kc n", kc=KC), s_st, waits=wmid)
                    st_mark = P.now(s_st)
                    wln2 = emit_ln(lambda kc, ring=ring: xsl(ring, kc), ln2_sb,
                                   wmid + P.now(s_pe, s_act))
                    xring_free[ring] = wln2 + st_mark
                    emit_router_logits(s, c, wln2)
                    emit_ln2_tok_store(s, c, wln2)
                    if nxt is not None:
                        loads[nxt] = emit_xload(*nxt)
                        rn = (nxt[0] * NCH + nxt[1]) % 2
                        wln = emit_ln(lambda kc, rn=rn: xsl(rn, kc), xln_sb,
                                      loads[nxt] + qfree)
                    else:
                        wln = None
                    if c == NCH - 1:
                        emit_routing_masks(s)
            emit_routing_pos(n_seq - 1)

            if dev_taps:
                wall = P.now(s_dve, s_act, s_pe)
                P.dma("sync", taps["t_logits"], logits_sb[:, 0:ng * NE], s_out,
                      waits=wall)
        ph = P.now(s_pe, s_act, s_dve, s_st, s_x, s_w)

        cap_e = n_seq * CAPS
        nblk = cap_e // 128
        with contextlib.ExitStack() as _fs:
            fent = _fs.enter_context
            w1t_sb = fent(nc.sbuf_tensor("w1t_sb", [128, KC * DFF], BF16))
            w2t_sb = fent(nc.sbuf_tensor("w2t_sb", [128, 32 * D], BF16))
            h_sb = fent(nc.sbuf_tensor("h_sb", [128, 32 * 512], BF16))
            xg_sb = fent(nc.sbuf_tensor("xg_sb", [128, 2 * 4 * D], BF16))
            xct_sb = fent(nc.sbuf_tensor("xct_sb", [128, KC * 512], BF16))
            ys_sb = fent(nc.sbuf_tensor("ys_sb", [128, 2 * D], BF16))
            b2r_sb = fent(nc.sbuf_tensor("b2r_sb", [1, 2 * D], BF16))
            gf_sb = fent(nc.sbuf_tensor("gf_sb", [nblk, 128], F32))
            gcol_sb = fent(nc.sbuf_tensor("gcol_sb", [128, nblk], F32))
            gi_sb = fent(nc.sbuf_tensor("gi_sb", [128, 2 * nblk], I32))
            gtok_sb = fent(nc.sbuf_tensor("gtok_sb", [128, 2 * nblk], I32))
            gsc_sb = fent(nc.sbuf_tensor("gsc_sb", [128, 2 * nblk], I32))
            gt2_sb = fent(nc.sbuf_tensor("gt2_sb", [128, nblk], I32))
            zz_sb = fent(nc.sbuf_tensor("zz_sb", [128, D], BF16))

            P.op("vector", lambda e: e.memset(zz_sb[:, :], 0.0), waits=ph,
                 inc=s_dve)
            wz = P.now(s_dve)
            nsl = (2 * tok + 128) // 128
            for b in range(nsl):
                P.dma("sync", slot_d[b * 128:(b + 1) * 128, :], zz_sb[:, :],
                      s_st, waits=wz if b == 0 else ())
            wslz = P.now(s_st)

            NCC = (nblk + 3) // 4
            h_mark = [()]

            def emit_wload(eidx, which, waits):
                if which == "w1lo":
                    P.dma("sync", w1t_sb[:, :].rearrange("p (kc m) -> p kc m", kc=KC)
                          [:, :, 0:DFF // 2],
                          w1bf[eidx].rearrange("(kc p) m -> p kc m", p=128)
                          [:, :, 0:DFF // 2], s_wf, waits=waits)
                elif which == "w1hi":
                    P.dma("sync", w1t_sb[:, :].rearrange("p (kc m) -> p kc m", kc=KC)
                          [:, :, DFF // 2:DFF],
                          w1bf[eidx].rearrange("(kc p) m -> p kc m", p=128)
                          [:, :, DFF // 2:DFF], s_wf, waits=waits)
                else:
                    P.dma("sync", w2t_sb[:, :].rearrange("p (kc m) -> p kc m", kc=32),
                          w2bf[eidx].rearrange("(kc p) m -> p kc m", p=128), s_wf,
                          waits=waits)
                return P.now(s_wf)

            def emit_gprep(eidx, gpar, waits):
                gofs = gpar * nblk
                P.dma("sync", gf_sb[:, :],
                      g_d[eidx * cap_e:(eidx + 1) * cap_e, 0:1]
                      .rearrange("(r n) one -> r (n one)", n=128), s_x,
                      waits=waits)
                P.op("tensor", lambda e: e.transpose(
                    p6[0:128, 0:nblk], gf_sb[:, :], ident_f[0:nblk, 0:nblk]),
                    waits=P.now(s_x) + bfree(6), inc=s_pe)
                P.op("vector", lambda e: e.tensor_copy(
                    gcol_sb[:, :], p6[0:128, 0:nblk]), waits=P.now(s_pe), inc=s_dve)
                bmark(6, s_dve)
                sl = slice(gofs, gofs + nblk)
                P.op("vector", lambda e, sl=sl: e.tensor_copy(
                    gi_sb[:, sl], gcol_sb[:, :]), waits=P.now(s_dve), inc=s_dve)
                P.op("vector", lambda e, sl=sl: e.tensor_scalar(
                    gtok_sb[:, sl], gi_sb[:, sl], 8191, None, OP.bitwise_and),
                    waits=P.now(s_dve), inc=s_dve)
                P.op("vector", lambda e, sl=sl: e.tensor_scalar(
                    gsc_sb[:, sl], gi_sb[:, sl], 13, 1, OP.logical_shift_right,
                    OP.bitwise_and), waits=P.now(s_dve), inc=s_dve)
                P.op("vector", lambda e, sl=sl: e.tensor_scalar(
                    gt2_sb[:, :], gtok_sb[:, sl], 2, None, OP.mult),
                    waits=P.now(s_dve), inc=s_dve)
                P.op("vector", lambda e, sl=sl: e.tensor_add(
                    gsc_sb[:, sl], gsc_sb[:, sl], gt2_sb[:, :]),
                    waits=P.now(s_dve), inc=s_dve)
                P.op("vector", lambda e, sl=sl: e.tensor_scalar(
                    gt2_sb[:, :], gi_sb[:, sl], 20, None, OP.logical_shift_right),
                    waits=P.now(s_dve), inc=s_dve)
                P.op("vector", lambda e: e.tensor_scalar(
                    gt2_sb[:, :], gt2_sb[:, :], 1 << 18, None, OP.mult),
                    waits=P.now(s_dve), inc=s_dve)
                P.op("vector", lambda e, sl=sl: e.tensor_add(
                    gsc_sb[:, sl], gsc_sb[:, sl], gt2_sb[:, :]),
                    waits=P.now(s_dve), inc=s_dve)
                P.op("vector", lambda e, sl=sl: e.tensor_scalar_min(
                    gsc_sb[:, sl], gsc_sb[:, sl], 2 * tok),
                    waits=P.now(s_dve), inc=s_dve)
                return P.now(s_dve)

            wfr_lo = {0: emit_wload(0, "w1lo", ph + cast_marks[0])}
            wfr_hi = {0: emit_wload(0, "w1hi", [])}
            wfr_w2 = {0: emit_wload(0, "w2", [])}
            widx_mark = {0: emit_gprep(0, 0, ph)}
            P.dma("gpsimd", b2r_sb[0:1, 0:D],
                  b2r[0].rearrange("(one d) -> one d", one=1), s_w, waits=ph)
            wb2r = {0: P.now(s_w)}
            g_mark = {}
            xg_tmark = [[], []]
            NJ = NE * NCC

            def emit_gather(j):
                ej = j // NCC
                ccj = j % NCC
                gofs_j = (ej % 2) * nblk
                xgo = (j % 2) * 4 * D
                for b in range(4):
                    blk_i = ccj * 4 + b
                    P.op("gpsimd", lambda e, b=b, blk_i=blk_i, xgo=xgo, gofs_j=gofs_j:
                         e.indirect_dma_start(
                             xg_sb[:, xgo + b * D: xgo + (b + 1) * D], None,
                             ln2tok_d,
                             IndirectOffsetOnAxis(
                                 ap=gtok_sb[:, gofs_j + blk_i: gofs_j + blk_i + 1],
                                 axis=0)),
                         waits=(widx_mark[ej] + list(xg_tmark[j % 2]))
                         if b == 0 else (), inc=s_g, inc_n=16)
                g_mark[j] = P.now(s_g)

            def emit_T(j):
                xgo = (j % 2) * 4 * D
                for b in range(4):
                    wb7 = bfree(7)
                    for kcc in range(KC):
                        P.op("tensor", lambda e, b=b, kcc=kcc, xgo=xgo: e.transpose(
                            p7[:, kcc * 128:(kcc + 1) * 128],
                            xg_sb[:, xgo + b * D + kcc * 128:
                                  xgo + b * D + (kcc + 1) * 128],
                            ident_bf[:, :]),
                            waits=(g_mark[j] + wb7) if kcc == 0 else (), inc=s_pe)
                    wtr = P.now(s_pe)
                    dst = xct_sb[:, :].rearrange(
                        "p (kc n) -> p kc n", kc=KC)[:, :, b * 128:(b + 1) * 128]
                    P.op("vector", lambda e, dst=dst: e.tensor_copy(
                        dst, p7[:, :].rearrange("p (kc n) -> p kc n", kc=KC)),
                        waits=wtr, inc=s_dve)
                    bmark(7, s_dve)
                xg_tmark[j % 2] = P.now(s_pe)
                return P.now(s_dve)

            emit_gather(0)
            emit_gather(1)
            wxc = emit_T(0)
            first_scatter = True

            for j in range(NJ):
                eidx = j // NCC
                cc = j % NCC
                gofs = (eidx % 2) * nblk
                if j + 2 < NJ:
                    emit_gather(j + 2)
                for m in range(32):
                    bi = m % 2
                    pb = PB[bi]
                    wb = bfree(bi)
                    w0 = []
                    if m == 0:
                        w0 = wxc + wfr_lo[eidx] + list(h_mark[0])
                    elif m == 16:
                        w0 = list(wfr_hi[eidx])
                    for kcc in range(KC):
                        P.op("tensor", lambda e, m=m, kcc=kcc, pb=pb: e.matmul(
                            pb[:, :],
                            w1t_sb[:, kcc * DFF + m * 128: kcc * DFF + (m + 1) * 128],
                            xct_sb[:, kcc * 512:(kcc + 1) * 512],
                            start=(kcc == 0), stop=(kcc == KC - 1)),
                            waits=(wb + w0) if kcc == 0 else (), inc=s_pe)
                    wmm = P.now(s_pe)
                    P.op("scalar", lambda e, m=m, pb=pb, eidx=eidx: e.activation(
                        h_sb[:, m * 512:(m + 1) * 512], pb[:, :], AF.Gelu,
                        bias=b1_sb[:, eidx * 32 + m: eidx * 32 + m + 1],
                        scale=1.0), waits=wmm, inc=s_act)
                    bmark(bi, s_act)
                    if cc == NCC - 1 and eidx + 1 < NE:
                        if m == 15:
                            wfr_lo[eidx + 1] = emit_wload(
                                eidx + 1, "w1lo", P.now(s_pe) + cast_marks[eidx + 1])
                        elif m == 31:
                            wfr_hi[eidx + 1] = emit_wload(
                                eidx + 1, "w1hi", P.now(s_pe))
                if cc == NCC - 3 and eidx + 1 < NE:
                    widx_mark[eidx + 1] = emit_gprep(
                        eidx + 1, 1 - eidx % 2, P.now(s_g, s_w, s_pe))
                if cc == NCC - 2 and eidx + 1 < NE:
                    po = ((eidx + 1) % 2) * D
                    P.dma("gpsimd", b2r_sb[0:1, po:po + D],
                          b2r[eidx + 1].rearrange("(one d) -> one d", one=1), s_w,
                          waits=P.now(s_pe))
                    wb2r[eidx + 1] = P.now(s_w)
                wxc_next = emit_T(j + 1) if j + 1 < NJ else []
                wh = P.now(s_act)
                for mp in range(4):
                    blk_i = cc * 4 + mp
                    bA = 2 + (mp % 2) * 2
                    wbA = bfree(bA) + bfree(bA + 1)
                    for m in range(32):
                        for half in range(2):
                            pb = PB[bA + half]
                            w0 = []
                            if m == 0 and half == 0:
                                w0 = wh + wbA
                                if mp == 0:
                                    w0 = w0 + list(wfr_w2[eidx])
                            P.op("tensor", lambda e, m=m, mp=mp, half=half, pb=pb:
                                 e.matmul(
                                     pb[:, :],
                                     h_sb[:, m * 512 + mp * 128: m * 512 + (mp + 1) * 128],
                                     w2t_sb[:, m * D + half * 512: m * D + (half + 1) * 512],
                                     start=(m == 0), stop=False),
                                 waits=w0, inc=s_pe)
                    for half in range(2):
                        pb = PB[bA + half]
                        po = (eidx % 2) * D
                        P.op("tensor", lambda e, half=half, pb=pb, po=po: e.matmul(
                            pb[:, :], ones_row_bf[0:1, :],
                            b2r_sb[0:1, po + half * 512: po + (half + 1) * 512],
                            start=False, stop=True),
                            waits=list(wb2r[eidx]) if half == 0 else (), inc=s_pe)
                    wmm = P.now(s_pe)
                    yr = mp % 2
                    for half in range(2):
                        pb = PB[bA + half]
                        P.op("scalar", lambda e, pb=pb, yr=yr, half=half: e.activation(
                            ys_sb[:, yr * D + half * 512: yr * D + (half + 1) * 512],
                            pb[:, :], AF.Copy, bias=0.0, scale=1.0),
                            waits=(wmm + P.now(s_w)) if half == 0 else (), inc=s_act)
                        bmark(bA + half, s_act)
                    wsc0 = wslz if first_scatter else []
                    first_scatter = False
                    P.op("gpsimd", lambda e, yr=yr, blk_i=blk_i, gofs=gofs:
                         e.indirect_dma_start(
                             slot_d,
                             IndirectOffsetOnAxis(
                                 ap=gsc_sb[:, gofs + blk_i: gofs + blk_i + 1],
                                 axis=0),
                             ys_sb[:, yr * D:(yr + 1) * D], None),
                         waits=P.now(s_act) + list(wsc0), inc=s_w, inc_n=16)
                h_mark[0] = P.now(s_pe)
                wxc = wxc_next
                if cc == NCC - 1 and eidx + 1 < NE:
                    wfr_w2[eidx + 1] = emit_wload(eidx + 1, "w2", P.now(s_pe))

        phf = P.now(s_pe, s_act, s_dve, s_w, s_x, s_wf, s_st)
        with contextlib.ExitStack() as _cs:
            cent = _cs.enter_context
            sl_sb = cent(nc.sbuf_tensor("sl_sb", [128, 4 * 2 * D], BF16))
            t0_sb = cent(nc.sbuf_tensor("t0_sb", [128, 4 * D], F32))
            tsum_sb = cent(nc.sbuf_tensor("tsum_sb", [128, 4 * D], BF16))
            fx_sb = cent(nc.sbuf_tensor("fx_sb", [128, 4 * D], F32))
            fo_sb = cent(nc.sbuf_tensor("fo_sb", [128, 4 * D], F32))
            fin_sb = cent(nc.sbuf_tensor("fin_sb", [1, 1], F32))
            RC = 4
            f_marks = [()] * RC
            fx_marks = [()] * RC
            for g in range(ng):
                s, rem = divmod(g, GSEQ)
                r = g % RC
                P.dma("sync", sl_sb[:, r * 2 * D:(r + 1) * 2 * D]
                      .rearrange("p (q d) -> p q d", q=2),
                      slot_d[2 * g * 128:2 * (g + 1) * 128, :]
                      .rearrange("(p q) d -> p q d", q=2),
                      s_x, waits=list(f_marks[r]) + (phf if g < RC else []))
                P.dma("scalar", fx_sb[:, r * D:(r + 1) * D]
                      .rearrange("p (kc n) -> p kc n", kc=KC),
                      xmid_d[s, :, rem * 128:(rem + 1) * 128]
                      .rearrange("(kc p) n -> p kc n", p=128),
                      s_g, waits=list(fx_marks[r]) + (phf if g < RC else []))
                wld = P.now(s_x, s_g)
                P.op("vector", lambda e, r=r, g=g: e.tensor_scalar_mul(
                    t0_sb[:, r * D:(r + 1) * D], sl_sb[:, r * 2 * D: r * 2 * D + D],
                    wv12_sb[:, g:g + 1]), waits=wld + (P.now(s_pe) if g < RC else []),
                    inc=s_dve)
                P.op("vector", lambda e, r=r, g=g: e.scalar_tensor_tensor(
                    tsum_sb[:, r * D:(r + 1) * D],
                    sl_sb[:, r * 2 * D + D: r * 2 * D + 2 * D],
                    wv12_sb[:, NG + g: NG + g + 1], t0_sb[:, r * D:(r + 1) * D],
                    OP.mult, OP.add), waits=P.now(s_dve), inc=s_dve)
                wts = P.now(s_dve)
                wb7 = bfree(7)
                for kcc in range(KC):
                    P.op("tensor", lambda e, r=r, kcc=kcc: e.transpose(
                        p7[:, kcc * 128:(kcc + 1) * 128],
                        tsum_sb[:, r * D + kcc * 128: r * D + (kcc + 1) * 128],
                        ident_bf[:, :]),
                        waits=(wts + wb7) if kcc == 0 else (), inc=s_pe)
                wtr = P.now(s_pe)
                P.op("vector", lambda e, r=r: e.tensor_add(
                    fo_sb[:, r * D:(r + 1) * D]
                    .rearrange("p (kc n) -> p kc n", kc=KC),
                    fx_sb[:, r * D:(r + 1) * D]
                    .rearrange("p (kc n) -> p kc n", kc=KC),
                    p7[:, :].rearrange("p (kc n) -> p kc n", kc=KC)),
                    waits=wtr, inc=s_dve)
                bmark(7, s_dve)
                fx_marks[r] = P.now(s_dve)
                P.dma("gpsimd", outT[s, :, rem * 128:(rem + 1) * 128]
                      .rearrange("(kc p) n -> p kc n", p=128),
                      fo_sb[:, r * D:(r + 1) * D]
                      .rearrange("p (kc n) -> p kc n", kc=KC),
                      s_out, waits=P.now(s_dve))
                f_marks[r] = P.now(s_out)
            if dev_taps:
                P.dma("gpsimd", taps["t_xmid"], xmid_d, s_out,
                      waits=[("s_st", P.cnt["s_st"])])
                P.dma("gpsimd", taps["t_g"],
                      g_d.rearrange("(p n) one -> p (n one)", p=128), s_out,
                      waits=phf)
            P.op("vector", lambda e: e.memset(fin_sb[:, :], 0.0),
                 waits=P.now(s_out), inc=s_dve)

        P.run(blk)
    P.finish()
    return nc


_NC_CACHE = {}


def _prep(inputs):
    wo = np.asarray(inputs["L0_attn_Wo"], np.float32)
    woPK = wo.T.reshape(NH, DH, D).transpose(1, 0, 2)
    woMC = np.ascontiguousarray(
        woPK.reshape(DH, NH, KC, 128).transpose(2, 0, 1, 3))
    common = dict(
        wqT=np.ascontiguousarray(np.asarray(inputs["L0_attn_Wq"], np.float32).T),
        wkT=np.ascontiguousarray(np.asarray(inputs["L0_attn_Wk"], np.float32).T),
        wvT=np.ascontiguousarray(np.asarray(inputs["L0_attn_Wv"], np.float32).T),
        woMC=woMC,
        bq=np.ascontiguousarray(np.asarray(inputs["L0_attn_bq"], np.float32)
                                .reshape(KC, 128).T),
        bk=np.ascontiguousarray(np.asarray(inputs["L0_attn_bk"], np.float32)
                                .reshape(KC, 128).T),
        bvC=np.ascontiguousarray(np.asarray(inputs["L0_attn_bv"], np.float32)
                                 .reshape(NH, DH).T),
        ltriBD=np.ascontiguousarray(np.kron(
            np.eye(NE, dtype=np.float32),
            np.triu(np.ones((16, 16), dtype=np.float32), k=1))),
        bo=np.ascontiguousarray(np.asarray(inputs["L0_attn_bo"], np.float32)
                                .reshape(KC, 128).T),
        wrT=np.ascontiguousarray(np.asarray(inputs["L0_ffn_Wr"], np.float32).T),
        w1T=np.ascontiguousarray(np.transpose(
            np.asarray(inputs["L0_ffn_W1"], np.float32), (0, 2, 1))),
        b1=np.ascontiguousarray(np.asarray(inputs["L0_ffn_b1"], np.float32)
                                .reshape(NE, 32, 128).transpose(2, 0, 1)),
        w2T=np.ascontiguousarray(np.transpose(
            np.asarray(inputs["L0_ffn_W2"], np.float32), (0, 2, 1))),
        b2=np.ascontiguousarray(np.asarray(inputs["L0_ffn_b2"], np.float32)
                                .reshape(NE, KC, 128).transpose(2, 0, 1)),
        b2r=np.ascontiguousarray(np.asarray(inputs["L0_ffn_b2"], np.float32)),
    )
    return common


def kernel(**inputs):
    if "full" not in _NC_CACHE:
        _NC_CACHE["full"] = build_core_program()
    nc = _NC_CACHE["full"]
    x = np.asarray(inputs["L0_attn_x"], dtype=np.float32)
    common = _prep(inputs)
    in_maps = []
    for cid in range(NCORES):
        xs = x[cid * NSEQ:(cid + 1) * NSEQ]
        m = dict(common)
        m["xT"] = np.ascontiguousarray(np.transpose(xs, (0, 2, 1)))
        in_maps.append(m)
    res = run_bass_kernel_spmd(nc, in_maps, core_ids=list(range(NCORES)))
    outs = []
    for cid in range(NCORES):
        oT = np.asarray(res.results[cid]["outT"])
        outs.append(np.transpose(oT, (0, 2, 1)))
    return np.ascontiguousarray(np.concatenate(outs, axis=0))


# version 13 (final)
# speedup vs baseline: 1.0589x; 1.0549x over previous
import contextlib
import numpy as np
import ml_dtypes
import concourse.bass as bass
import concourse.mybir as mybir
from concourse.bass_core_rust import IndirectOffsetOnAxis
from concourse.bass_utils import run_bass_kernel_spmd

F32 = mybir.dt.float32
BF16 = mybir.dt.bfloat16
I32 = mybir.dt.int32
AF = mybir.ActivationFunctionType
OP = mybir.AluOpType
AX = mybir.AxisListType

B, L, D = 32, 2048, 1024
NCORES = 8
NSEQ = B // NCORES
W = 512
NCH = L // W
NH, DH = 16, 64
DFF, NE = 4096, 8
QSCALE = 0.125
EPS = 1e-5
KC = D // 128
TOK = NSEQ * L
NG = TOK // 128
GSEQ = L // 128
VROW = NH * 65
CAPS = 1152
CAPE = 4 * CAPS
NBLK = CAPE // 128
GMAGIC = float(1 << 20)
FFN_MODE = "sparse"

DEV_TAPS = False


class Prog:

    ENGINES = ("sync", "scalar", "vector", "tensor", "gpsimd")

    def __init__(self, nc):
        self.nc = nc
        self.ops = {e: [] for e in self.ENGINES}
        self.cnt = {}
        self.sems = {}
        self._ctx = []

    def sem(self, name):
        ctx = self.nc.semaphore(name)
        self.sems[name] = ctx.__enter__()
        self._ctx.append(ctx)
        self.cnt[name] = 0
        return name

    def now(self, *names):
        return [(n, self.cnt[n]) for n in names]

    def op(self, eng, build, waits=(), inc=None, inc_n=1):
        waits = tuple((s, v) for (s, v) in waits if v > 0)
        sem = self.sems[inc] if inc is not None else None

        def fn(e, build=build, waits=waits, sem=sem, inc_n=inc_n):
            for (s, v) in waits:
                e.wait_ge(self.sems[s], v)
            ins = build(e)
            if sem is not None:
                ins.then_inc(sem, inc_n)

        if inc is not None:
            self.cnt[inc] += inc_n
        self.ops[eng].append(fn)

    def dma(self, eng, out, in_, sem, waits=(), **kw):
        self.op(eng, lambda e: e.dma_start(out, in_, **kw), waits=waits,
                inc=sem, inc_n=16)

    def run(self, block):
        for name in self.ENGINES:
            if self.ops[name]:
                body = self.ops[name]

                def make(body=body):
                    def _(eng):
                        for fn in body:
                            fn(eng)
                    return _

                getattr(block, name)(make())

    def finish(self):
        for ctx in reversed(self._ctx):
            ctx.__exit__(None, None, None)


def build_core_program(n_seq=NSEQ, dev_taps=DEV_TAPS):
    nc = bass.Bass("TRN2", target_bir_lowering=False)
    tok = n_seq * L
    ng = tok // 128

    xT = nc.dram_tensor("xT", [n_seq, D, L], F32, kind="ExternalInput").ap()
    wqT = nc.dram_tensor("wqT", [D, D], F32, kind="ExternalInput").ap()
    wkT = nc.dram_tensor("wkT", [D, D], F32, kind="ExternalInput").ap()
    wvT = nc.dram_tensor("wvT", [D, D], F32, kind="ExternalInput").ap()
    woMC = nc.dram_tensor("woMC", [KC, DH, NH, 128], F32, kind="ExternalInput").ap()
    bq = nc.dram_tensor("bq", [128, KC], F32, kind="ExternalInput").ap()
    bk = nc.dram_tensor("bk", [128, KC], F32, kind="ExternalInput").ap()
    bvC = nc.dram_tensor("bvC", [64, NH], F32, kind="ExternalInput").ap()
    bo = nc.dram_tensor("bo", [128, KC], F32, kind="ExternalInput").ap()
    ltriBD = nc.dram_tensor("ltriBD", [128, 128], F32, kind="ExternalInput").ap()
    wrT = nc.dram_tensor("wrT", [D, NE], F32, kind="ExternalInput").ap()
    w1T = nc.dram_tensor("w1T", [NE, D, DFF], F32, kind="ExternalInput").ap()
    b1 = nc.dram_tensor("b1", [128, NE, 32], F32, kind="ExternalInput").ap()
    w2T = nc.dram_tensor("w2T", [NE, DFF, D], F32, kind="ExternalInput").ap()
    b2 = nc.dram_tensor("b2", [128, NE, KC], F32, kind="ExternalInput").ap()
    b2r = nc.dram_tensor("b2r", [NE, D], F32, kind="ExternalInput").ap()
    outT = nc.dram_tensor("outT", [n_seq, D, L], F32, kind="ExternalOutput").ap()

    xmid_d = nc.dram_tensor("xmid_d", [n_seq, D, L], F32).ap()
    ln2tok_d = nc.dram_tensor("ln2tok_d", [tok, D], BF16).ap()
    wobf = nc.dram_tensor("wobf", [KC, DH, NH, 128], BF16).ap()
    w1bf = nc.dram_tensor("w1bf", [NE, D, DFF], BF16).ap()
    w2bf = nc.dram_tensor("w2bf", [NE, DFF, D], BF16).ap()
    g_d = nc.dram_tensor("g_d", [NE * n_seq * CAPS, 1], F32).ap()
    slot_d = nc.dram_tensor("slot_d", [2 * tok + 128, D], BF16).ap()

    taps = {}
    if dev_taps:
        taps["t_xmid"] = nc.dram_tensor("t_xmid", [n_seq, D, L], F32,
                                        kind="ExternalOutput").ap()
        taps["t_logits"] = nc.dram_tensor("t_logits", [128, ng * NE], F32,
                                          kind="ExternalOutput").ap()
        taps["t_wt"] = nc.dram_tensor("t_wt", [128, NE * NG], F32,
                                      kind="ExternalOutput").ap()
        taps["t_g"] = nc.dram_tensor("t_g", [128, NE * n_seq * CAPS // 128], F32,
                                     kind="ExternalOutput").ap()

    P = Prog(nc)
    s_x = P.sem("s_x")
    s_w = P.sem("s_w")
    s_gp = P.sem("s_gp")
    s_wc = P.sem("s_wc")
    s_woc = P.sem("s_woc")
    s_wf = P.sem("s_wf")
    s_g = P.sem("s_g")
    s_pe = P.sem("s_pe")
    s_act = P.sem("s_act")
    s_dve = P.sem("s_dve")
    s_st = P.sem("s_st")
    s_out = P.sem("s_out")

    bank_marks = [()] * 8

    with contextlib.ExitStack() as _es:
        ent = _es.enter_context
        ident_f = ent(nc.sbuf_tensor("ident_f", [128, 128], F32))
        ident_bf = ent(nc.sbuf_tensor("ident_bf", [128, 128], BF16))
        ones_col_bf = ent(nc.sbuf_tensor("ones_col_bf", [128, 1], BF16))
        ones_row_bf = ent(nc.sbuf_tensor("ones_row_bf", [1, 128], BF16))
        b1_sb = ent(nc.sbuf_tensor("b1_sb", [128, NE * 32], F32))
        b2_sb = ent(nc.sbuf_tensor("b2_sb", [128, NE * KC], F32))
        st_sb = ent(nc.sbuf_tensor("st_sb", [1, 3 * W], F32))
        stbf_sb = ent(nc.sbuf_tensor("stbf_sb", [1, 2 * W + 4], BF16))
        sm_sb = ent(nc.sbuf_tensor("sm_sb", [128, 4], F32))
        eps_sb = ent(nc.sbuf_tensor("eps_sb", [128, 1], F32))
        wv12_sb = ent(nc.sbuf_tensor("wv12_sb", [128, 2 * NG], F32))
        PBh = [ent(nc.psum_tensor(f"p{i}", [128, 512], F32)) for i in range(7)]
        p7 = ent(nc.psum_tensor("p7", [128, 1024], BF16))
        p0, p1, p2, p3, p4, p5, p6 = PBh
        blk = ent(nc.Block())
        (lambda: None)()
        PB = PBh

        def bfree(i):
            return list(bank_marks[i])

        def bmark(i, sem):
            bank_marks[i] = P.now(sem)

        nc.const_aps.register(F32, EPS, eps_sb[:, 0:1])
        P.op("vector", lambda e: e.memset(eps_sb[:, :], EPS), inc=s_dve)
        P.op("vector", lambda e: e.memset(ident_f[:, :], 1.0), inc=s_dve)
        P.op("gpsimd", lambda e: e.affine_select(
            ident_f[:, :], ident_f[:, :], [[-1, 128]], OP.is_equal, 0.0,
            base=0, channel_multiplier=1), waits=P.now(s_dve), inc=s_gp, inc_n=1)
        P.op("vector", lambda e: e.memset(ident_bf[:, :], 1.0), inc=s_dve)
        P.op("gpsimd", lambda e: e.affine_select(
            ident_bf[:, :], ident_bf[:, :], [[-1, 128]], OP.is_equal, 0.0,
            base=0, channel_multiplier=1), waits=P.now(s_dve), inc=s_gp, inc_n=1)
        P.op("vector", lambda e: e.memset(ones_col_bf[:, :], 1.0), inc=s_dve)
        P.op("vector", lambda e: e.memset(ones_row_bf[:, :], 1.0), inc=s_dve)
        P.dma("gpsimd", wobf, woMC, s_woc)
        wocast_mark = P.now(s_woc)
        for eidx in range(NE):
            P.dma("gpsimd", w1bf[eidx], w1T[eidx], s_wc)
            P.dma("gpsimd", w2bf[eidx], w2T[eidx], s_wc)
        cast_marks = {e: P.now(s_wc) for e in range(NE)}
        P.dma("sync", b1_sb[:, :].rearrange("p (e mc) -> p e mc", e=NE), b1, s_x)
        P.dma("sync", b2_sb[:, :].rearrange("p (e mc) -> p e mc", e=NE), b2, s_x)

        with contextlib.ExitStack() as _as:
            aent = _as.enter_context
            wq_sb = aent(nc.sbuf_tensor("wq_sb", [128, KC * D], BF16))
            wk_sb = aent(nc.sbuf_tensor("wk_sb", [128, KC * D], BF16))
            wv_sb = aent(nc.sbuf_tensor("wv_sb", [128, KC * D], BF16))
            wos_sb = aent(nc.sbuf_tensor("wos_sb", [64, 2 * NH * 128], BF16))
            wr_sb = aent(nc.sbuf_tensor("wr_sb", [128, KC * NE], BF16))
            bq_sb = aent(nc.sbuf_tensor("bq_sb", [128, KC], F32))
            bk_sb = aent(nc.sbuf_tensor("bk_sb", [128, KC], F32))
            bo_sb = aent(nc.sbuf_tensor("bo_sb", [128, KC], F32))
            bvc_sb = aent(nc.sbuf_tensor("bvc_sb", [64, NH], F32))
            x_sb = aent(nc.sbuf_tensor("x_sb", [128, 2 * KC * W], F32))
            xbf_sb = aent(nc.sbuf_tensor("xbf_sb", [128, KC * W], BF16))
            a_sb = aent(nc.sbuf_tensor("a_sb", [128, 2 * KC * W], BF16))
            xln_sb = aent(nc.sbuf_tensor("xln_sb", [128, KC * W], BF16))
            k_sb = aent(nc.sbuf_tensor("k_sb", [128, 2 * KC * W], BF16))
            v_sb = aent(nc.sbuf_tensor("v_sb", [128, 2 * 4 * VROW], BF16))
            o_sb = aent(nc.sbuf_tensor("o_sb", [64, NH * W], BF16))
            rbc_sb = aent(nc.sbuf_tensor("rbc_sb", [64, W], BF16))
            lt2_sb = aent(nc.sbuf_tensor("lt2_sb", [128, 2 * 1024], BF16))
            logits_sb = aent(nc.sbuf_tensor("logits_sb", [128, NG * NE], F32))
            rt_sb = aent(nc.sbuf_tensor("rt_sb", [128, 5 * 128 + 64], F32))
            ltbd_sb = aent(nc.sbuf_tensor("ltbd_sb", [128, 128], BF16))
            ptri_sb = aent(nc.sbuf_tensor("ptri_sb", [128, 128], F32))
            evals_sb = aent(nc.sbuf_tensor("evals_sb", [128, NE], F32))
            ids_sb = aent(nc.sbuf_tensor("ids_sb", [128, NG], F32))
            iti_sb = aent(nc.sbuf_tensor("iti_sb", [128, NG], I32))
            posall_sb = aent(nc.sbuf_tensor("posall_sb", [128, 16 * NE], F32))
            fe_all = aent(nc.sbuf_tensor("fe_all", [128, 128], BF16))
            ft_all = aent(nc.sbuf_tensor("ft_all", [128, 128], BF16))
            ct_all = aent(nc.sbuf_tensor("ct_all", [128, 128], F32))
            pm_all = aent(nc.sbuf_tensor("pm_all", [128, 128], F32))
            tot_sb = aent(nc.sbuf_tensor("tot_sb", [128, NE], F32))
            bcol8_sb = aent(nc.sbuf_tensor("bcol8_sb", [128, NE], F32))
            offi_sb = aent(nc.sbuf_tensor("offi_sb", [128, 32], I32))
            valb_sb = aent(nc.sbuf_tensor("valb_sb", [128, 16], F32))
            q_sb = xbf_sb
            ln2_sb = xln_sb

            for wsb, wdr in ((wq_sb, wqT), (wk_sb, wkT), (wv_sb, wvT)):
                P.dma("gpsimd", wsb[:, :].rearrange("p (kc m) -> p kc m", kc=KC),
                      wdr.rearrange("(kc p) m -> p kc m", p=128), s_w)
            P.dma("gpsimd", wr_sb[:, :].rearrange("p (kc e) -> p kc e", kc=KC),
                  wrT.rearrange("(kc p) e -> p kc e", p=128), s_w)
            for bsb, bdr in ((bq_sb, bq), (bk_sb, bk), (bo_sb, bo), (bvc_sb, bvC)):
                P.dma("sync", bsb[:, :], bdr, s_x)
            P.op("vector", lambda e: e.tensor_scalar_mul(
                bq_sb[:, :], bq_sb[:, :], QSCALE), waits=P.now(s_x), inc=s_dve)
            for r in range(2):
                ap = v_sb[:, r * 4 * VROW:(r + 1) * 4 * VROW].rearrange(
                    "p (x d) -> p x d", d=65)[:, :, 64:65]
                P.op("vector", lambda e, ap=ap: e.memset(ap, 1.0), inc=s_dve)
            P.dma("gpsimd", ltbd_sb[:, :], ltriBD, s_w)
            P.op("vector", lambda e: e.memset(ptri_sb[:, :], 1.0), inc=s_dve)
            P.op("gpsimd", lambda e: e.affine_select(
                ptri_sb[:, :], ptri_sb[:, :], [[1, 128]], OP.is_ge, 0.0,
                base=-1, channel_multiplier=-1), waits=P.now(s_dve), inc=s_gp, inc_n=1)
            P.op("gpsimd", lambda e: e.iota(iti_sb[:, 0:NE], [[1, NE]], base=0,
                                            channel_multiplier=0), inc=s_gp, inc_n=1)
            P.op("vector", lambda e: e.tensor_copy(evals_sb[:, :], iti_sb[:, 0:NE]),
                 waits=P.now(s_gp), inc=s_dve)
            P.op("gpsimd", lambda e: e.iota(iti_sb[:, :], [[128, NG]], base=0,
                                            channel_multiplier=1),
                 waits=P.now(s_dve), inc=s_gp, inc_n=1)
            P.op("vector", lambda e: e.tensor_copy(ids_sb[:, :], iti_sb[:, :]),
                 waits=P.now(s_gp), inc=s_dve)
            ginit_cols = NE * n_seq * CAPS // 128
            P.op("vector", lambda e: e.memset(logits_sb[:, 0:ginit_cols], GMAGIC),
                 inc=s_dve)
            P.dma("sync", g_d.rearrange("(p n) one -> p (n one)", p=128),
                  logits_sb[:, 0:ginit_cols], s_st, waits=P.now(s_dve))
            wginit = P.now(s_st)
            attw = P.now(s_w, s_gp, s_x, s_dve, s_pe)

            def xsl(ring, kc, a=0, b=W):
                off = ring * KC * W + kc * W
                return x_sb[:, off + a: off + b]

            xring_free = [[], []]

            def emit_xload(s, c):
                ring = (s * NCH + c) % 2
                w = list(xring_free[ring])
                P.dma("sync", x_sb[:, ring * KC * W:(ring + 1) * KC * W]
                      .rearrange("p (kc n) -> p kc n", kc=KC),
                      xT[s, :, c * W:(c + 1) * W].rearrange("(kc p) n -> p kc n", p=128),
                      s_x, waits=w)
                return P.now(s_x)

            loads = {(0, 0): emit_xload(0, 0)}

            def emit_ln(src_slices, out_sb, wsrc, wrows=None):
                if wrows is None:
                    wrows = P.now(s_act, s_pe)
                for kc in range(KC):
                    P.op("vector", lambda e, kc=kc: e.tensor_copy(
                        xbf_sb[:, kc * W:(kc + 1) * W], src_slices(kc)),
                        waits=wsrc if kc == 0 else (), inc=s_dve)
                wcp = P.now(s_dve)
                for kc in range(KC):
                    P.op("tensor", lambda e, kc=kc: e.matmul(
                        p5[0:1, :], ones_col_bf[:, 0:1],
                        xbf_sb[:, kc * W:(kc + 1) * W],
                        start=(kc == 0), stop=(kc == KC - 1)),
                        waits=(wcp + bfree(5)) if kc == 0 else (), inc=s_pe)
                wsum = P.now(s_pe)
                for kc in range(KC):
                    P.op("vector", lambda e, kc=kc: e.tensor_mul(
                        xbf_sb[:, kc * W:(kc + 1) * W],
                        xbf_sb[:, kc * W:(kc + 1) * W],
                        xbf_sb[:, kc * W:(kc + 1) * W]),
                        waits=wsum if kc == 0 else (), inc=s_dve)
                wsq = P.now(s_dve)
                for kc in range(KC):
                    P.op("tensor", lambda e, kc=kc: e.matmul(
                        p6[0:1, :], ones_col_bf[:, 0:1],
                        xbf_sb[:, kc * W:(kc + 1) * W],
                        start=(kc == 0), stop=(kc == KC - 1)),
                        waits=(wsq + bfree(6)) if kc == 0 else (), inc=s_pe)
                wst = P.now(s_pe)
                mrow = st_sb[0:1, 0:W]
                vrow = st_sb[0:1, W:2 * W]
                rrow = st_sb[0:1, 2 * W:3 * W]
                P.op("vector", lambda e: e.tensor_scalar_mul(mrow, p5[0:1, :], 1.0 / D),
                     waits=wst + wrows, inc=s_dve)
                P.op("vector", lambda e: e.tensor_scalar_mul(vrow, p6[0:1, :], 1.0 / D),
                     inc=s_dve)
                bmark(5, s_dve)
                bmark(6, s_dve)
                P.op("vector", lambda e: e.tensor_mul(rrow, mrow, mrow),
                     waits=P.now(s_dve), inc=s_dve)
                P.op("vector", lambda e: e.tensor_sub(vrow, vrow, rrow),
                     waits=P.now(s_dve), inc=s_dve)
                P.op("scalar", lambda e: e.activation(rrow, vrow, AF.Ln,
                                                      bias=EPS, scale=1.0),
                     waits=P.now(s_dve), inc=s_act)
                P.op("scalar", lambda e: e.activation(rrow, rrow, AF.Exp,
                                                      bias=0.0, scale=-0.5),
                     waits=P.now(s_act), inc=s_act)
                mbf = stbf_sb[0:1, 0:W]
                rbf = stbf_sb[0:1, W:2 * W]
                wr2 = P.now(s_act)
                P.op("vector", lambda e: e.tensor_copy(mbf, mrow), waits=wr2, inc=s_dve)
                P.op("vector", lambda e: e.tensor_copy(rbf, rrow), inc=s_dve)
                wb_ = P.now(s_dve)
                P.op("tensor", lambda e: e.matmul(p5[:, :], ones_row_bf[0:1, :], mbf,
                                                  start=True, stop=True),
                     waits=wb_ + bfree(5), inc=s_pe)
                P.op("tensor", lambda e: e.matmul(p6[:, :], ones_row_bf[0:1, :], rbf,
                                                  start=True, stop=True),
                     waits=bfree(6), inc=s_pe)
                wbc = P.now(s_pe)
                for kc in range(KC):
                    P.op("vector", lambda e, kc=kc: e.tensor_sub(
                        xbf_sb[:, kc * W:(kc + 1) * W], src_slices(kc), p5[:, :]),
                        waits=wbc if kc == 0 else (), inc=s_dve)
                    P.op("vector", lambda e, kc=kc: e.tensor_mul(
                        out_sb[:, kc * W:(kc + 1) * W],
                        xbf_sb[:, kc * W:(kc + 1) * W], p6[:, :]),
                        waits=P.now(s_dve), inc=s_dve)
                bmark(5, s_dve)
                bmark(6, s_dve)
                return P.now(s_dve)

            def emit_proj_fm(wsb, bias_sb, out_sb, wln, scale):
                for mc in range(KC):
                    pb = PB[mc % 2]
                    wb = bfree(mc % 2)
                    for kcc in range(KC):
                        P.op("tensor", lambda e, mc=mc, kcc=kcc, pb=pb: e.matmul(
                            pb[:, :],
                            wsb[:, kcc * D + mc * 128: kcc * D + (mc + 1) * 128],
                            xln_sb[:, kcc * W:(kcc + 1) * W],
                            start=(kcc == 0), stop=(kcc == KC - 1)),
                            waits=(wln + wb + attw) if kcc == 0 else (), inc=s_pe)
                    wmm = P.now(s_pe)
                    P.op("scalar", lambda e, mc=mc, pb=pb: e.activation(
                        out_sb[:, mc * W:(mc + 1) * W], pb[:, :], AF.Identity,
                        bias=bias_sb[:, mc:mc + 1], scale=scale),
                        waits=wmm, inc=s_act)
                    bmark(mc % 2, s_act)
                return P.now(s_act)

            def emit_proj_v(vr, wln):
                base = vr * 4 * VROW
                for tc in range(4):
                    for half in range(2):
                        bi = (tc * 2 + half) % 2
                        pb = PB[bi]
                        wb = bfree(bi)
                        for kcc in range(KC):
                            P.op("tensor", lambda e, tc=tc, half=half, kcc=kcc, pb=pb: e.matmul(
                                pb[:, :],
                                xln_sb[:, kcc * W + tc * 128: kcc * W + (tc + 1) * 128],
                                wv_sb[:, kcc * D + half * 512: kcc * D + (half + 1) * 512],
                                start=(kcc == 0), stop=(kcc == KC - 1)),
                                waits=(wln + wb + attw) if kcc == 0 else (), inc=s_pe)
                        wmm = P.now(s_pe)
                        off = base + tc * VROW + half * 8 * 65
                        dst = v_sb[:, off: off + 8 * 65].rearrange(
                            "p (s d) -> p s d", d=65)[:, :, 0:64]
                        P.op("vector", lambda e, pb=pb, dst=dst: e.tensor_copy(
                            dst, pb[:, :].rearrange("p (s d) -> p s d", d=64)),
                            waits=wmm, inc=s_dve)
                        bmark(bi, s_dve)
                return P.now(s_dve)

            def emit_attention(c, negc_ready):
                rings = [c % 2] if c == 0 else [(c - 1) % 2, c % 2]
                nsub = 4 * len(rings)
                wqkv = P.now(s_act, s_dve)
                exp_done = {}
                av_done = {}

                def escore(h):
                    nonlocal negc_ready
                    hb = (h % 2) * 64
                    hc = h // 2
                    aoff = (h % 2) * KC * W
                    for j in range(nsub):
                        ring = rings[j // 4]
                        tj = j % 4
                        bi = 1 + (h * nsub + j) % 3
                        pb = PB[bi]
                        wb = bfree(bi)
                        P.op("tensor", lambda e, ring=ring, tj=tj, pb=pb, hb=hb, hc=hc: e.matmul(
                            pb[:, :],
                            k_sb[hb:hb + 64, ring * KC * W + hc * W + tj * 128:
                                 ring * KC * W + hc * W + (tj + 1) * 128],
                            q_sb[hb:hb + 64, hc * W:(hc + 1) * W],
                            start=True, stop=True),
                            waits=wqkv + wb, inc=s_pe)
                        wmm = P.now(s_pe)
                        if negc_ready is None:
                            P.op("vector", lambda e, pb=pb: e.reduce_max(
                                sm_sb[:, 0:1], pb[:, :], axis=AX.X),
                                waits=wmm, inc=s_dve)
                            P.op("tensor", lambda e: e.transpose(
                                p5[0:1, 0:128], sm_sb[:, 0:1], ident_f[:, :]),
                                waits=P.now(s_dve) + bfree(5), inc=s_pe)
                            P.op("vector", lambda e: e.reduce_max(
                                sm_sb[0:1, 1:2], p5[0:1, 0:128], axis=AX.X),
                                waits=P.now(s_pe), inc=s_dve)
                            bmark(5, s_dve)
                            P.op("vector", lambda e: e.tensor_scalar_mul(
                                sm_sb[0:1, 1:2], sm_sb[0:1, 1:2], -1.0),
                                waits=P.now(s_dve), inc=s_dve)
                            P.op("vector", lambda e: e.tensor_copy(
                                stbf_sb[0:1, 2 * W:2 * W + 1], sm_sb[0:1, 1:2]),
                                waits=P.now(s_dve), inc=s_dve)
                            P.op("tensor", lambda e: e.matmul(
                                p5[:, 0:1], ones_row_bf[0:1, :],
                                stbf_sb[0:1, 2 * W:2 * W + 1],
                                start=True, stop=True),
                                waits=P.now(s_dve) + bfree(5), inc=s_pe)
                            P.op("vector", lambda e: e.tensor_copy(
                                sm_sb[:, 2:3], p5[:, 0:1]),
                                waits=P.now(s_pe), inc=s_dve)
                            bmark(5, s_dve)
                            negc_ready = P.now(s_dve)
                        P.op("scalar", lambda e, j=j, pb=pb, aoff=aoff: e.activation(
                            a_sb[:, aoff + j * W: aoff + (j + 1) * W], pb[:, :],
                            AF.Exp, bias=sm_sb[:, 2:3], scale=1.0),
                            waits=P.now(s_pe) + negc_ready, inc=s_act)
                        bmark(bi, s_act)
                    exp_done[h] = P.now(s_act)

                def eav(h):
                    bi = 4 + h % 3
                    pbav = PB[bi]
                    wavb = bfree(bi)
                    aoff = (h % 2) * KC * W
                    for j in range(nsub):
                        ring = rings[j // 4]
                        tj = j % 4
                        voff = ring * 4 * VROW + tj * VROW + h * 65
                        P.op("tensor", lambda e, j=j, voff=voff, pbav=pbav, aoff=aoff:
                             e.matmul(
                                 pbav[0:65, :], v_sb[:, voff: voff + 65],
                                 a_sb[:, aoff + j * W: aoff + (j + 1) * W],
                                 start=(j == 0), stop=(j == nsub - 1)),
                             waits=(exp_done[h] + wavb + attw) if j == 0 else (),
                             inc=s_pe)
                    av_done[h] = P.now(s_pe)

                def etail(h):
                    bi = 4 + h % 3
                    pbav = PB[bi]
                    par = h % 2
                    lr = st_sb[0:1, par * W:(par + 1) * W]
                    er = stbf_sb[0:1, par * W:(par + 1) * W]
                    P.op("scalar", lambda e, pbav=pbav, lr=lr: e.activation(
                        lr, pbav[64:65, :], AF.Ln, bias=0.0, scale=1.0),
                        waits=av_done[h], inc=s_act)
                    P.op("scalar", lambda e, lr=lr, er=er: e.activation(
                        er, lr, AF.Exp, bias=0.0, scale=-1.0),
                        waits=P.now(s_act), inc=s_act)
                    P.op("tensor", lambda e, er=er: e.matmul(
                        p0[0:64, :], ones_row_bf[0:1, 0:64], er,
                        start=True, stop=True),
                        waits=P.now(s_act) + bfree(0), inc=s_pe)
                    P.op("vector", lambda e: e.tensor_copy(rbc_sb[:, :], p0[0:64, :]),
                         waits=P.now(s_pe), inc=s_dve)
                    bmark(0, s_dve)
                    P.op("vector", lambda e, h=h, pbav=pbav: e.tensor_mul(
                        o_sb[0:64, h * W:(h + 1) * W], pbav[0:64, :], rbc_sb[:, :]),
                        waits=P.now(s_dve), inc=s_dve)
                    P.op("vector", lambda e, h=h: e.tensor_scalar_add(
                        o_sb[0:64, h * W:(h + 1) * W],
                        o_sb[0:64, h * W:(h + 1) * W], bvc_sb[0:64, h:h + 1]),
                        waits=P.now(s_dve), inc=s_dve)
                    bmark(bi, s_dve)

                escore(0)
                escore(1)
                eav(0)
                for h in range(2, NH):
                    escore(h)
                    eav(h - 1)
                    etail(h - 2)
                qfree = P.now(s_pe)
                eav(NH - 1)
                etail(NH - 2)
                etail(NH - 1)
                return P.now(s_dve), negc_ready, qfree

            wos_marks = [(), ()]

            def emit_outproj_residual(ring, watt):
                wslab = {}
                for mc in range(2):
                    P.dma("sync", wos_sb[:, (mc % 2) * NH * 128:(mc % 2 + 1) * NH * 128]
                          .rearrange("d (h n) -> d h n", h=NH), wobf[mc],
                          s_x, waits=list(wos_marks[mc % 2]) + list(wocast_mark))
                    wslab[mc] = P.now(s_x)
                for mc in range(KC):
                    bi = mc % 2
                    pb = PB[bi]
                    wb = bfree(bi)
                    for h in range(NH):
                        P.op("tensor", lambda e, mc=mc, h=h, pb=pb: e.matmul(
                            pb[:, :],
                            wos_sb[0:64, (mc % 2) * NH * 128 + h * 128:
                                   (mc % 2) * NH * 128 + (h + 1) * 128],
                            o_sb[0:64, h * W:(h + 1) * W],
                            start=(h == 0), stop=(h == NH - 1)),
                            waits=(watt + wb + wslab[mc]) if h == 0 else (), inc=s_pe)
                    wos_marks[mc % 2] = P.now(s_pe)
                    if mc + 2 < KC:
                        P.dma("sync", wos_sb[:, (mc % 2) * NH * 128:(mc % 2 + 1) * NH * 128]
                              .rearrange("d (h n) -> d h n", h=NH), wobf[mc + 2],
                              s_x, waits=list(wos_marks[mc % 2]))
                        wslab[mc + 2] = P.now(s_x)
                    wmm = P.now(s_pe)
                    P.op("vector", lambda e, mc=mc, pb=pb, ring=ring: e.scalar_tensor_tensor(
                        xsl(ring, mc), pb[:, :], bo_sb[:, mc:mc + 1],
                        xsl(ring, mc), OP.add, OP.add), waits=wmm, inc=s_dve)
                    bmark(bi, s_dve)
                return P.now(s_dve)

            def emit_router_logits(s, c, wln2):
                wb = bfree(5)
                for tc in range(4):
                    for kcc in range(KC):
                        P.op("tensor", lambda e, tc=tc, kcc=kcc: e.matmul(
                            p5[:, tc * NE:(tc + 1) * NE],
                            ln2_sb[:, kcc * W + tc * 128: kcc * W + (tc + 1) * 128],
                            wr_sb[:, kcc * NE:(kcc + 1) * NE],
                            start=(kcc == 0), stop=(kcc == KC - 1)),
                            waits=(wln2 + wb + attw) if (tc == 0 and kcc == 0) else (),
                            inc=s_pe)
                wmm = P.now(s_pe)
                g0 = s * GSEQ + c * 4
                wg0 = wginit if (s == 0 and c == 0) else []
                P.op("vector", lambda e, g0=g0: e.tensor_copy(
                    logits_sb[:, g0 * NE:(g0 + 4) * NE], p5[:, 0:4 * NE]),
                    waits=wmm + list(wg0), inc=s_dve)
                bmark(5, s_dve)

            lt2_marks = [(), ()]

            def emit_ln2_tok_store(s, c, wln2):
                for tc in range(4):
                    lr = tc % 2
                    wb = bfree(7)
                    for kcc in range(KC):
                        P.op("tensor", lambda e, tc=tc, kcc=kcc: e.transpose(
                            p7[:, kcc * 128:(kcc + 1) * 128],
                            ln2_sb[:, kcc * W + tc * 128: kcc * W + (tc + 1) * 128],
                            ident_bf[:, :]),
                            waits=(wln2 + wb) if kcc == 0 else (), inc=s_pe)
                    wtr = P.now(s_pe)
                    P.op("vector", lambda e, lr=lr: e.tensor_copy(
                        lt2_sb[:, lr * 1024:(lr + 1) * 1024], p7[:, :]),
                        waits=wtr + list(lt2_marks[lr]), inc=s_dve)
                    bmark(7, s_dve)
                    r0 = s * L + c * W + tc * 128
                    P.dma("sync", ln2tok_d[r0:r0 + 128, :],
                          lt2_sb[:, lr * 1024:(lr + 1) * 1024], s_st,
                          waits=P.now(s_dve))
                    lt2_marks[lr] = P.now(s_st)

            RT_D, RT_EQ, RT_MK, RT_SEL, RT_S = 0, 128, 256, 384, 512
            RT_M1, RT_M2D, RT_W1, RT_W2 = 640, 656, 672, 688

            def rt(a, n=128):
                return rt_sb[:, a:a + n]

            def rt3(a, n=128):
                return rt_sb[:, a:a + n].rearrange("p (g e) -> p g e", e=NE)

            def rtb(a):
                return rt_sb[:, a:a + 16].rearrange("p (g one) -> p g one", one=1) \
                    .broadcast_to([128, 16, NE])

            def emit_routing_masks(s):
                g0 = s * GSEQ
                l3 = logits_sb[:, g0 * NE:(g0 + GSEQ) * NE].rearrange(
                    "p (g e) -> p g e", e=NE)
                P.op("vector", lambda e, l3=l3: e.tensor_reduce(
                    rt(RT_M1, 16).rearrange("p (g one) -> p g one", one=1),
                    l3, AX.X, OP.max), waits=P.now(s_pe), inc=s_dve)
                P.op("vector", lambda e, l3=l3: e.tensor_sub(
                    rt3(RT_D), l3, rtb(RT_M1)), waits=P.now(s_dve), inc=s_dve)
                P.op("vector", lambda e: e.tensor_single_scalar(
                    rt(RT_EQ), rt(RT_D), 0.0, OP.is_ge),
                    waits=P.now(s_dve), inc=s_dve)
                P.op("vector", lambda e: e.scalar_tensor_tensor(
                    rt(RT_MK), rt(RT_EQ), -1e9, rt(RT_D), OP.mult, OP.add),
                    waits=P.now(s_dve), inc=s_dve)
                P.op("vector", lambda e: e.tensor_reduce(
                    rt(RT_M2D, 16).rearrange("p (g one) -> p g one", one=1),
                    rt3(RT_MK), AX.X, OP.max), waits=P.now(s_dve), inc=s_dve)
                P.op("scalar", lambda e: e.activation(
                    rt(RT_W1, 16), rt(RT_M2D, 16), AF.Exp, bias=0.0, scale=1.0),
                    waits=P.now(s_dve), inc=s_act)
                P.op("vector", lambda e: e.tensor_scalar_add(
                    rt(RT_W1, 16), rt(RT_W1, 16), 1.0), waits=P.now(s_act), inc=s_dve)
                P.op("vector", lambda e: e.reciprocal(
                    rt(RT_W1, 16), rt(RT_W1, 16)), waits=P.now(s_dve), inc=s_dve)
                P.op("vector", lambda e: e.tensor_scalar(
                    rt(RT_W2, 16), rt(RT_W1, 16), -1.0, 1.0, OP.mult, OP.add),
                    waits=P.now(s_dve), inc=s_dve)
                P.op("vector", lambda e: e.tensor_tensor(
                    rt3(RT_SEL), rt3(RT_D), rtb(RT_M2D), OP.is_ge),
                    waits=P.now(s_dve), inc=s_dve)
                P.op("vector", lambda e: e.tensor_sub(
                    rt(RT_S), rt(RT_SEL), rt(RT_EQ)), waits=P.now(s_dve), inc=s_dve)
                P.op("vector", lambda e, s=s: e.tensor_copy(
                    wv12_sb[:, s * 16:(s + 1) * 16], rt(RT_W1, 16)),
                    waits=P.now(s_dve), inc=s_dve)
                P.op("vector", lambda e, s=s: e.tensor_copy(
                    wv12_sb[:, NG + s * 16: NG + (s + 1) * 16], rt(RT_W2, 16)),
                    inc=s_dve)
                P.op("vector", lambda e: e.tensor_copy(
                    fe_all[:, :].rearrange("p (e g) -> p e g", g=GSEQ),
                    rt3(RT_SEL).transpose([0, 2, 1])),
                    waits=P.now(s_dve) + P.now(s_pe), inc=s_dve)

            def emit_routing_pos(s):
                P.op("tensor", lambda e: e.transpose(
                    p7[:, 0:128], fe_all[:, :], ident_bf[:, :]),
                    waits=P.now(s_dve) + bfree(7), inc=s_pe)
                P.op("vector", lambda e: e.tensor_copy(
                    ft_all[:, :], p7[:, 0:128]), waits=P.now(s_pe), inc=s_dve)
                bmark(7, s_dve)
                P.op("tensor", lambda e: e.matmul(
                    p5[:, 0:128], ltbd_sb[:, :], ft_all[:, :],
                    start=True, stop=True),
                    waits=P.now(s_dve) + bfree(5), inc=s_pe)
                P.op("vector", lambda e: e.tensor_copy(
                    ct_all[:, :], p5[:, 0:128]), waits=P.now(s_pe), inc=s_dve)
                bmark(5, s_dve)
                P.op("tensor", lambda e: e.transpose(
                    p6[:, 0:128], ct_all[:, :], ident_f[:, :]),
                    waits=P.now(s_dve) + bfree(6), inc=s_pe)
                P.op("vector", lambda e: e.tensor_copy(
                    pm_all[:, :], p6[:, 0:128]), waits=P.now(s_pe), inc=s_dve)
                bmark(6, s_dve)
                P.op("vector", lambda e: e.tensor_reduce(
                    tot_sb[:, :].rearrange("p (e one) -> p e one", one=1),
                    rt3(RT_SEL).transpose([0, 2, 1]), AX.X, OP.add),
                    waits=P.now(s_dve), inc=s_dve)
                P.op("tensor", lambda e: e.matmul(
                    p5[:, 0:NE], ptri_sb[:, :], tot_sb[:, :],
                    start=True, stop=True),
                    waits=P.now(s_dve) + bfree(5), inc=s_pe)
                P.op("vector", lambda e: e.tensor_copy(
                    bcol8_sb[:, :], p5[:, 0:NE]), waits=P.now(s_pe), inc=s_dve)
                bmark(5, s_dve)
                P.op("vector", lambda e: e.tensor_add(
                    posall_sb[:, :].rearrange("p (g e) -> p g e", e=NE),
                    pm_all[:, :].rearrange("p (e g) -> p e g", g=GSEQ)
                    .transpose([0, 2, 1]),
                    bcol8_sb[:, :].rearrange("p (one e) -> p one e", one=1)
                    .broadcast_to([128, GSEQ, NE])),
                    waits=P.now(s_dve), inc=s_dve)
                pos3 = posall_sb[:, :].rearrange("p (g e) -> p g e", e=NE)
                eb = evals_sb[:, :].rearrange("p (one e) -> p one e", one=1) \
                    .broadcast_to([128, 16, NE])
                SC = RT_D
                for i, (msk, dst) in enumerate(
                        ((RT_EQ, SC), (RT_EQ, SC + 16), (RT_S, SC + 32),
                         (RT_S, SC + 48))):
                    src_b = eb if i % 2 == 0 else pos3
                    P.op("vector", lambda e, msk=msk, src_b=src_b: e.tensor_mul(
                        rt3(RT_MK), rt3(msk), src_b),
                        waits=P.now(s_dve), inc=s_dve)
                    P.op("vector", lambda e, dst=dst: e.tensor_reduce(
                        rt(dst, 16).rearrange("p (g one) -> p g one", one=1),
                        rt3(RT_MK), AX.X, OP.add), waits=P.now(s_dve), inc=s_dve)
                for dst in (SC + 16, SC + 48):
                    P.op("vector", lambda e, dst=dst: e.tensor_scalar_min(
                        rt(dst, 16), rt(dst, 16), float(CAPS - 1)),
                        waits=P.now(s_dve), inc=s_dve)
                for ei, pi, oi in ((SC, SC + 16, 0), (SC + 32, SC + 48, 16)):
                    P.op("vector", lambda e, ei=ei, s=s: e.tensor_scalar(
                        rt(ei, 16), rt(ei, 16), float(n_seq), float(s),
                        OP.mult, OP.add), waits=P.now(s_dve), inc=s_dve)
                    P.op("vector", lambda e, ei=ei: e.tensor_scalar_mul(
                        rt(ei, 16), rt(ei, 16), float(CAPS)),
                        waits=P.now(s_dve), inc=s_dve)
                    P.op("vector", lambda e, ei=ei, pi=pi: e.tensor_add(
                        rt(ei, 16), rt(ei, 16), rt(pi, 16)),
                        waits=P.now(s_dve), inc=s_dve)
                    P.op("vector", lambda e, ei=ei, oi=oi: e.tensor_copy(
                        offi_sb[:, oi:oi + 16], rt(ei, 16)),
                        waits=P.now(s_dve) + P.now(s_w), inc=s_dve)
                P.op("vector", lambda e, s=s: e.tensor_scalar_add(
                    valb_sb[:, :], ids_sb[:, s * 16:(s + 1) * 16], 8192.0),
                    waits=P.now(s_dve) + P.now(s_w), inc=s_dve)
                woff = P.now(s_dve)
                for g in range(16):
                    P.op("gpsimd", lambda e, g=g, s=s: e.indirect_dma_start(
                        g_d, IndirectOffsetOnAxis(ap=offi_sb[:, g:g + 1], axis=0),
                        ids_sb[:, s * 16 + g: s * 16 + g + 1], None),
                        waits=(woff + P.now(s_st)) if g == 0 else (),
                        inc=s_w, inc_n=16)
                    P.op("gpsimd", lambda e, g=g: e.indirect_dma_start(
                        g_d, IndirectOffsetOnAxis(ap=offi_sb[:, 16 + g:17 + g], axis=0),
                        valb_sb[:, g:g + 1], None),
                        inc=s_w, inc_n=16)

            wln = None
            for s in range(n_seq):
                negc_ready = None
                for c in range(NCH):
                    ring = (s * NCH + c) % 2
                    nxt = (s, c + 1) if c + 1 < NCH else (
                        (s + 1, 0) if s + 1 < n_seq else None)
                    if wln is None:
                        loads_cur = loads.pop((s, c))
                        wln = emit_ln(lambda kc, ring=ring: xsl(ring, kc), xln_sb,
                                      loads_cur + P.now(s_pe, s_act, s_dve, s_st))
                    kr = c % 2
                    for mc in range(KC):
                        pb = PB[mc % 2]
                        wb = bfree(mc % 2)
                        for kcc in range(KC):
                            P.op("tensor", lambda e, mc=mc, kcc=kcc, pb=pb: e.matmul(
                                pb[:, :],
                                wk_sb[:, kcc * D + mc * 128: kcc * D + (mc + 1) * 128],
                                xln_sb[:, kcc * W:(kcc + 1) * W],
                                start=(kcc == 0), stop=(kcc == KC - 1)),
                                waits=(wln + wb + attw + P.now(s_pe)) if kcc == 0 else (),
                                inc=s_pe)
                        wmm = P.now(s_pe)
                        P.op("scalar", lambda e, mc=mc, pb=pb, kr=kr: e.activation(
                            k_sb[:, kr * KC * W + mc * W: kr * KC * W + (mc + 1) * W],
                            pb[:, :], AF.Identity, bias=bk_sb[:, mc:mc + 1], scale=1.0),
                            waits=wmm, inc=s_act)
                        bmark(mc % 2, s_act)
                    emit_proj_v(kr, wln)
                    emit_proj_fm(wq_sb, bq_sb, q_sb, wln + P.now(s_dve), QSCALE)
                    watt, negc_ready, qfree = emit_attention(c, negc_ready)
                    if c == 0 and s > 0:
                        emit_routing_pos(s - 1)
                    wmid = emit_outproj_residual(ring, watt)
                    P.dma("sync", xmid_d[s, :, c * W:(c + 1) * W]
                          .rearrange("(kc p) n -> p kc n", p=128),
                          x_sb[:, ring * KC * W:(ring + 1) * KC * W]
                          .rearrange("p (kc n) -> p kc n", kc=KC), s_st, waits=wmid)
                    st_mark = P.now(s_st)
                    wln2 = emit_ln(lambda kc, ring=ring: xsl(ring, kc), ln2_sb,
                                   wmid + P.now(s_pe, s_act))
                    xring_free[ring] = wln2 + st_mark
                    emit_router_logits(s, c, wln2)
                    emit_ln2_tok_store(s, c, wln2)
                    if nxt is not None:
                        loads[nxt] = emit_xload(*nxt)
                        rn = (nxt[0] * NCH + nxt[1]) % 2
                        wln = emit_ln(lambda kc, rn=rn: xsl(rn, kc), xln_sb,
                                      loads[nxt] + qfree)
                    else:
                        wln = None
                    if c == NCH - 1:
                        emit_routing_masks(s)
            emit_routing_pos(n_seq - 1)

            if dev_taps:
                wall = P.now(s_dve, s_act, s_pe)
                P.dma("sync", taps["t_logits"], logits_sb[:, 0:ng * NE], s_out,
                      waits=wall)
        ph = P.now(s_pe, s_act, s_dve, s_st, s_x, s_w)

        cap_e = n_seq * CAPS
        nblk = cap_e // 128
        with contextlib.ExitStack() as _fs:
            fent = _fs.enter_context
            w1t_sb = fent(nc.sbuf_tensor("w1t_sb", [128, KC * DFF], BF16))
            w2t_sb = fent(nc.sbuf_tensor("w2t_sb", [128, 32 * D], BF16))
            h_sb = fent(nc.sbuf_tensor("h_sb", [128, 32 * 512], BF16))
            xg_sb = fent(nc.sbuf_tensor("xg_sb", [128, 2 * 4 * D], BF16))
            xct_sb = fent(nc.sbuf_tensor("xct_sb", [128, KC * 512], BF16))
            ys_sb = fent(nc.sbuf_tensor("ys_sb", [128, 2 * D], BF16))
            b2r_sb = fent(nc.sbuf_tensor("b2r_sb", [1, 2 * D], BF16))
            gf_sb = fent(nc.sbuf_tensor("gf_sb", [nblk, 128], F32))
            gcol_sb = fent(nc.sbuf_tensor("gcol_sb", [128, nblk], F32))
            gi_sb = fent(nc.sbuf_tensor("gi_sb", [128, 2 * nblk], I32))
            gtok_sb = fent(nc.sbuf_tensor("gtok_sb", [128, 2 * nblk], I32))
            gsc_sb = fent(nc.sbuf_tensor("gsc_sb", [128, 2 * nblk], I32))
            gt2_sb = fent(nc.sbuf_tensor("gt2_sb", [128, nblk], I32))
            zz_sb = fent(nc.sbuf_tensor("zz_sb", [128, D], BF16))

            P.op("vector", lambda e: e.memset(zz_sb[:, :], 0.0), waits=ph,
                 inc=s_dve)
            wz = P.now(s_dve)
            nsl = (2 * tok + 128) // 128
            for b in range(nsl):
                P.dma("sync", slot_d[b * 128:(b + 1) * 128, :], zz_sb[:, :],
                      s_st, waits=wz if b == 0 else ())
            wslz = P.now(s_st)

            NCC = (nblk + 3) // 4
            h_mark = [()]

            def emit_wload(eidx, which, waits):
                if which == "w1lo":
                    P.dma("sync", w1t_sb[:, :].rearrange("p (kc m) -> p kc m", kc=KC)
                          [:, :, 0:DFF // 2],
                          w1bf[eidx].rearrange("(kc p) m -> p kc m", p=128)
                          [:, :, 0:DFF // 2], s_wf, waits=waits)
                elif which == "w1hi":
                    P.dma("sync", w1t_sb[:, :].rearrange("p (kc m) -> p kc m", kc=KC)
                          [:, :, DFF // 2:DFF],
                          w1bf[eidx].rearrange("(kc p) m -> p kc m", p=128)
                          [:, :, DFF // 2:DFF], s_wf, waits=waits)
                else:
                    P.dma("sync", w2t_sb[:, :].rearrange("p (kc m) -> p kc m", kc=32),
                          w2bf[eidx].rearrange("(kc p) m -> p kc m", p=128), s_wf,
                          waits=waits)
                return P.now(s_wf)

            def emit_gprep(eidx, gpar, waits):
                gofs = gpar * nblk
                P.dma("sync", gf_sb[:, :],
                      g_d[eidx * cap_e:(eidx + 1) * cap_e, 0:1]
                      .rearrange("(r n) one -> r (n one)", n=128), s_x,
                      waits=waits)
                P.op("tensor", lambda e: e.transpose(
                    p6[0:128, 0:nblk], gf_sb[:, :], ident_f[0:nblk, 0:nblk]),
                    waits=P.now(s_x) + bfree(6), inc=s_pe)
                P.op("vector", lambda e: e.tensor_copy(
                    gcol_sb[:, :], p6[0:128, 0:nblk]), waits=P.now(s_pe), inc=s_dve)
                bmark(6, s_dve)
                sl = slice(gofs, gofs + nblk)
                P.op("vector", lambda e, sl=sl: e.tensor_copy(
                    gi_sb[:, sl], gcol_sb[:, :]), waits=P.now(s_dve), inc=s_dve)
                P.op("vector", lambda e, sl=sl: e.tensor_scalar(
                    gtok_sb[:, sl], gi_sb[:, sl], 8191, None, OP.bitwise_and),
                    waits=P.now(s_dve), inc=s_dve)
                P.op("vector", lambda e, sl=sl: e.tensor_scalar(
                    gsc_sb[:, sl], gi_sb[:, sl], 13, 1, OP.logical_shift_right,
                    OP.bitwise_and), waits=P.now(s_dve), inc=s_dve)
                P.op("vector", lambda e, sl=sl: e.tensor_scalar(
                    gt2_sb[:, :], gtok_sb[:, sl], 2, None, OP.mult),
                    waits=P.now(s_dve), inc=s_dve)
                P.op("vector", lambda e, sl=sl: e.tensor_add(
                    gsc_sb[:, sl], gsc_sb[:, sl], gt2_sb[:, :]),
                    waits=P.now(s_dve), inc=s_dve)
                P.op("vector", lambda e, sl=sl: e.tensor_scalar(
                    gt2_sb[:, :], gi_sb[:, sl], 20, None, OP.logical_shift_right),
                    waits=P.now(s_dve), inc=s_dve)
                P.op("vector", lambda e: e.tensor_scalar(
                    gt2_sb[:, :], gt2_sb[:, :], 1 << 18, None, OP.mult),
                    waits=P.now(s_dve), inc=s_dve)
                P.op("vector", lambda e, sl=sl: e.tensor_add(
                    gsc_sb[:, sl], gsc_sb[:, sl], gt2_sb[:, :]),
                    waits=P.now(s_dve), inc=s_dve)
                P.op("vector", lambda e, sl=sl: e.tensor_scalar_min(
                    gsc_sb[:, sl], gsc_sb[:, sl], 2 * tok),
                    waits=P.now(s_dve), inc=s_dve)
                return P.now(s_dve)

            wfr_lo = {0: emit_wload(0, "w1lo", ph + cast_marks[0])}
            wfr_hi = {0: emit_wload(0, "w1hi", [])}
            wfr_w2 = {0: emit_wload(0, "w2", [])}
            widx_mark = {0: emit_gprep(0, 0, ph)}
            P.dma("gpsimd", b2r_sb[0:1, 0:D],
                  b2r[0].rearrange("(one d) -> one d", one=1), s_w, waits=ph)
            wb2r = {0: P.now(s_w)}
            g_mark = {}
            xg_tmark = [[], []]
            NJ = NE * NCC

            def emit_gather(j):
                ej = j // NCC
                ccj = j % NCC
                gofs_j = (ej % 2) * nblk
                xgo = (j % 2) * 4 * D
                for b in range(4):
                    blk_i = ccj * 4 + b
                    P.op("gpsimd", lambda e, b=b, blk_i=blk_i, xgo=xgo, gofs_j=gofs_j:
                         e.indirect_dma_start(
                             xg_sb[:, xgo + b * D: xgo + (b + 1) * D], None,
                             ln2tok_d,
                             IndirectOffsetOnAxis(
                                 ap=gtok_sb[:, gofs_j + blk_i: gofs_j + blk_i + 1],
                                 axis=0)),
                         waits=(widx_mark[ej] + list(xg_tmark[j % 2]))
                         if b == 0 else (), inc=s_g, inc_n=16)
                g_mark[j] = P.now(s_g)

            def emit_T(j):
                xgo = (j % 2) * 4 * D
                for b in range(4):
                    wb7 = bfree(7)
                    for kcc in range(KC):
                        P.op("tensor", lambda e, b=b, kcc=kcc, xgo=xgo: e.transpose(
                            p7[:, kcc * 128:(kcc + 1) * 128],
                            xg_sb[:, xgo + b * D + kcc * 128:
                                  xgo + b * D + (kcc + 1) * 128],
                            ident_bf[:, :]),
                            waits=(g_mark[j] + wb7) if kcc == 0 else (), inc=s_pe)
                    wtr = P.now(s_pe)
                    dst = xct_sb[:, :].rearrange(
                        "p (kc n) -> p kc n", kc=KC)[:, :, b * 128:(b + 1) * 128]
                    P.op("vector", lambda e, dst=dst: e.tensor_copy(
                        dst, p7[:, :].rearrange("p (kc n) -> p kc n", kc=KC)),
                        waits=wtr, inc=s_dve)
                    bmark(7, s_dve)
                xg_tmark[j % 2] = P.now(s_pe)
                return P.now(s_dve)

            emit_gather(0)
            emit_gather(1)
            wxc = emit_T(0)
            first_scatter = True

            for j in range(NJ):
                eidx = j // NCC
                cc = j % NCC
                gofs = (eidx % 2) * nblk
                if j + 2 < NJ:
                    emit_gather(j + 2)
                for m in range(32):
                    bi = m % 2
                    pb = PB[bi]
                    wb = bfree(bi)
                    w0 = []
                    if m == 0:
                        w0 = wxc + wfr_lo[eidx] + list(h_mark[0])
                    elif m == 16:
                        w0 = list(wfr_hi[eidx])
                    for kcc in range(KC):
                        P.op("tensor", lambda e, m=m, kcc=kcc, pb=pb: e.matmul(
                            pb[:, :],
                            w1t_sb[:, kcc * DFF + m * 128: kcc * DFF + (m + 1) * 128],
                            xct_sb[:, kcc * 512:(kcc + 1) * 512],
                            start=(kcc == 0), stop=(kcc == KC - 1)),
                            waits=(wb + w0) if kcc == 0 else (), inc=s_pe)
                    wmm = P.now(s_pe)
                    P.op("scalar", lambda e, m=m, pb=pb, eidx=eidx: e.activation(
                        h_sb[:, m * 512:(m + 1) * 512], pb[:, :], AF.Gelu,
                        bias=b1_sb[:, eidx * 32 + m: eidx * 32 + m + 1],
                        scale=1.0), waits=wmm, inc=s_act)
                    bmark(bi, s_act)
                    if cc == NCC - 1 and eidx + 1 < NE:
                        if m == 15:
                            wfr_lo[eidx + 1] = emit_wload(
                                eidx + 1, "w1lo", P.now(s_pe) + cast_marks[eidx + 1])
                        elif m == 31:
                            wfr_hi[eidx + 1] = emit_wload(
                                eidx + 1, "w1hi", P.now(s_pe))
                if cc == NCC - 3 and eidx + 1 < NE:
                    widx_mark[eidx + 1] = emit_gprep(
                        eidx + 1, 1 - eidx % 2, P.now(s_g, s_w, s_pe))
                if cc == NCC - 2 and eidx + 1 < NE:
                    po = ((eidx + 1) % 2) * D
                    P.dma("gpsimd", b2r_sb[0:1, po:po + D],
                          b2r[eidx + 1].rearrange("(one d) -> one d", one=1), s_w,
                          waits=P.now(s_pe))
                    wb2r[eidx + 1] = P.now(s_w)
                wxc_next = emit_T(j + 1) if j + 1 < NJ else []
                wh = P.now(s_act)
                for mp in range(4):
                    blk_i = cc * 4 + mp
                    bA = 2 + (mp % 2) * 2
                    wbA = bfree(bA) + bfree(bA + 1)
                    for m in range(32):
                        for half in range(2):
                            pb = PB[bA + half]
                            w0 = []
                            if m == 0 and half == 0:
                                w0 = wh + wbA
                                if mp == 0:
                                    w0 = w0 + list(wfr_w2[eidx])
                            P.op("tensor", lambda e, m=m, mp=mp, half=half, pb=pb:
                                 e.matmul(
                                     pb[:, :],
                                     h_sb[:, m * 512 + mp * 128: m * 512 + (mp + 1) * 128],
                                     w2t_sb[:, m * D + half * 512: m * D + (half + 1) * 512],
                                     start=(m == 0), stop=False),
                                 waits=w0, inc=s_pe)
                    for half in range(2):
                        pb = PB[bA + half]
                        po = (eidx % 2) * D
                        P.op("tensor", lambda e, half=half, pb=pb, po=po: e.matmul(
                            pb[:, :], ones_row_bf[0:1, :],
                            b2r_sb[0:1, po + half * 512: po + (half + 1) * 512],
                            start=False, stop=True),
                            waits=list(wb2r[eidx]) if half == 0 else (), inc=s_pe)
                    wmm = P.now(s_pe)
                    yr = mp % 2
                    for half in range(2):
                        pb = PB[bA + half]
                        P.op("scalar", lambda e, pb=pb, yr=yr, half=half: e.activation(
                            ys_sb[:, yr * D + half * 512: yr * D + (half + 1) * 512],
                            pb[:, :], AF.Copy, bias=0.0, scale=1.0),
                            waits=(wmm + P.now(s_w)) if half == 0 else (), inc=s_act)
                        bmark(bA + half, s_act)
                    wsc0 = wslz if first_scatter else []
                    first_scatter = False
                    P.op("gpsimd", lambda e, yr=yr, blk_i=blk_i, gofs=gofs:
                         e.indirect_dma_start(
                             slot_d,
                             IndirectOffsetOnAxis(
                                 ap=gsc_sb[:, gofs + blk_i: gofs + blk_i + 1],
                                 axis=0),
                             ys_sb[:, yr * D:(yr + 1) * D], None),
                         waits=P.now(s_act) + list(wsc0), inc=s_w, inc_n=16)
                h_mark[0] = P.now(s_pe)
                wxc = wxc_next
                if cc == NCC - 1 and eidx + 1 < NE:
                    wfr_w2[eidx + 1] = emit_wload(eidx + 1, "w2", P.now(s_pe))

        phf = P.now(s_pe, s_act, s_dve, s_w, s_x, s_wf, s_st)
        with contextlib.ExitStack() as _cs:
            cent = _cs.enter_context
            sl_sb = cent(nc.sbuf_tensor("sl_sb", [128, 4 * 2 * D], BF16))
            t0_sb = cent(nc.sbuf_tensor("t0_sb", [128, 4 * D], F32))
            tsum_sb = cent(nc.sbuf_tensor("tsum_sb", [128, 4 * D], BF16))
            fx_sb = cent(nc.sbuf_tensor("fx_sb", [128, 4 * D], F32))
            fo_sb = cent(nc.sbuf_tensor("fo_sb", [128, 4 * D], F32))
            fin_sb = cent(nc.sbuf_tensor("fin_sb", [1, 1], F32))
            RC = 4
            f_marks = [()] * RC
            fx_marks = [()] * RC
            for g in range(ng):
                s, rem = divmod(g, GSEQ)
                r = g % RC
                P.dma("sync", sl_sb[:, r * 2 * D:(r + 1) * 2 * D]
                      .rearrange("p (q d) -> p q d", q=2),
                      slot_d[2 * g * 128:2 * (g + 1) * 128, :]
                      .rearrange("(p q) d -> p q d", q=2),
                      s_x, waits=list(f_marks[r]) + (phf if g < RC else []))
                P.dma("scalar", fx_sb[:, r * D:(r + 1) * D]
                      .rearrange("p (kc n) -> p kc n", kc=KC),
                      xmid_d[s, :, rem * 128:(rem + 1) * 128]
                      .rearrange("(kc p) n -> p kc n", p=128),
                      s_g, waits=list(fx_marks[r]) + (phf if g < RC else []))
                wld = P.now(s_x, s_g)
                P.op("vector", lambda e, r=r, g=g: e.tensor_scalar_mul(
                    t0_sb[:, r * D:(r + 1) * D], sl_sb[:, r * 2 * D: r * 2 * D + D],
                    wv12_sb[:, g:g + 1]), waits=wld + (P.now(s_pe) if g < RC else []),
                    inc=s_dve)
                P.op("vector", lambda e, r=r, g=g: e.scalar_tensor_tensor(
                    tsum_sb[:, r * D:(r + 1) * D],
                    sl_sb[:, r * 2 * D + D: r * 2 * D + 2 * D],
                    wv12_sb[:, NG + g: NG + g + 1], t0_sb[:, r * D:(r + 1) * D],
                    OP.mult, OP.add), waits=P.now(s_dve), inc=s_dve)
                wts = P.now(s_dve)
                wb7 = bfree(7)
                for kcc in range(KC):
                    P.op("tensor", lambda e, r=r, kcc=kcc: e.transpose(
                        p7[:, kcc * 128:(kcc + 1) * 128],
                        tsum_sb[:, r * D + kcc * 128: r * D + (kcc + 1) * 128],
                        ident_bf[:, :]),
                        waits=(wts + wb7) if kcc == 0 else (), inc=s_pe)
                wtr = P.now(s_pe)
                P.op("vector", lambda e, r=r: e.tensor_add(
                    fo_sb[:, r * D:(r + 1) * D]
                    .rearrange("p (kc n) -> p kc n", kc=KC),
                    fx_sb[:, r * D:(r + 1) * D]
                    .rearrange("p (kc n) -> p kc n", kc=KC),
                    p7[:, :].rearrange("p (kc n) -> p kc n", kc=KC)),
                    waits=wtr, inc=s_dve)
                bmark(7, s_dve)
                fx_marks[r] = P.now(s_dve)
                P.dma("gpsimd", outT[s, :, rem * 128:(rem + 1) * 128]
                      .rearrange("(kc p) n -> p kc n", p=128),
                      fo_sb[:, r * D:(r + 1) * D]
                      .rearrange("p (kc n) -> p kc n", kc=KC),
                      s_out, waits=P.now(s_dve))
                f_marks[r] = P.now(s_out)
            if dev_taps:
                P.dma("gpsimd", taps["t_xmid"], xmid_d, s_out,
                      waits=[("s_st", P.cnt["s_st"])])
                P.dma("gpsimd", taps["t_g"],
                      g_d.rearrange("(p n) one -> p (n one)", p=128), s_out,
                      waits=phf)
            P.op("vector", lambda e: e.memset(fin_sb[:, :], 0.0),
                 waits=P.now(s_out), inc=s_dve)

        P.run(blk)
    P.finish()
    return nc


_NC_CACHE = {}


def _prep(inputs):
    wo = np.asarray(inputs["L0_attn_Wo"], np.float32)
    woPK = wo.T.reshape(NH, DH, D).transpose(1, 0, 2)
    woMC = np.ascontiguousarray(
        woPK.reshape(DH, NH, KC, 128).transpose(2, 0, 1, 3))
    common = dict(
        wqT=np.ascontiguousarray(np.asarray(inputs["L0_attn_Wq"], np.float32).T),
        wkT=np.ascontiguousarray(np.asarray(inputs["L0_attn_Wk"], np.float32).T),
        wvT=np.ascontiguousarray(np.asarray(inputs["L0_attn_Wv"], np.float32).T),
        woMC=woMC,
        bq=np.ascontiguousarray(np.asarray(inputs["L0_attn_bq"], np.float32)
                                .reshape(KC, 128).T),
        bk=np.ascontiguousarray(np.asarray(inputs["L0_attn_bk"], np.float32)
                                .reshape(KC, 128).T),
        bvC=np.ascontiguousarray(np.asarray(inputs["L0_attn_bv"], np.float32)
                                 .reshape(NH, DH).T),
        ltriBD=np.ascontiguousarray(np.kron(
            np.eye(NE, dtype=np.float32),
            np.triu(np.ones((16, 16), dtype=np.float32), k=1))),
        bo=np.ascontiguousarray(np.asarray(inputs["L0_attn_bo"], np.float32)
                                .reshape(KC, 128).T),
        wrT=np.ascontiguousarray(np.asarray(inputs["L0_ffn_Wr"], np.float32).T),
        w1T=np.ascontiguousarray(np.transpose(
            np.asarray(inputs["L0_ffn_W1"], np.float32), (0, 2, 1))),
        b1=np.ascontiguousarray(np.asarray(inputs["L0_ffn_b1"], np.float32)
                                .reshape(NE, 32, 128).transpose(2, 0, 1)),
        w2T=np.ascontiguousarray(np.transpose(
            np.asarray(inputs["L0_ffn_W2"], np.float32), (0, 2, 1))),
        b2=np.ascontiguousarray(np.asarray(inputs["L0_ffn_b2"], np.float32)
                                .reshape(NE, KC, 128).transpose(2, 0, 1)),
        b2r=np.ascontiguousarray(np.asarray(inputs["L0_ffn_b2"], np.float32)),
    )
    return common


def kernel(**inputs):
    if "full" not in _NC_CACHE:
        _NC_CACHE["full"] = build_core_program()
    nc = _NC_CACHE["full"]
    x = np.asarray(inputs["L0_attn_x"], dtype=np.float32)
    common = _prep(inputs)
    in_maps = []
    for cid in range(NCORES):
        xs = x[cid * NSEQ:(cid + 1) * NSEQ]
        m = dict(common)
        m["xT"] = np.ascontiguousarray(np.transpose(xs, (0, 2, 1)))
        in_maps.append(m)
    res = run_bass_kernel_spmd(nc, in_maps, core_ids=list(range(NCORES)))
    outs = []
    for cid in range(NCORES):
        oT = np.asarray(res.results[cid]["outT"])
        outs.append(np.transpose(oT, (0, 2, 1)))
    return np.ascontiguousarray(np.concatenate(outs, axis=0))
